# version 01 (final)
# baseline (speedup 1.0000x reference)
import os
import numpy as np
import concourse.bass as bass
import concourse.mybir as mybir
from concourse.bass_utils import run_bass_kernel_spmd
from contextlib import ExitStack

F32 = mybir.dt.float32
BF16 = mybir.dt.bfloat16
I32 = mybir.dt.int32
AF = mybir.ActivationFunctionType
ALU = mybir.AluOpType
AX = mybir.AxisListType

P = 128
T = 4096
D = 1024
NB = 2
TILE = 512
NTL = T // TILE
DC = D // P
HID = 4096
HC = HID // P
RC = 12582912.0
KTRUNC = int(os.environ.get("KTRUNC", "0"))
KPREP = int(os.environ.get("KPREP", "0"))
KEPOCH = int(os.environ.get("KEPOCH", "1"))
KPBITS = int(os.environ.get("KPBITS", "0"))
EPS = 1e-5

DIN = 2048
DINC = DIN // P
CONVD = 2304
CONVC = CONVD // P
NH = 32
HD = 64
DST = 128
DIP = 4384
QC = 128
NCC = TILE // QC

ENGS = ("pe", "act", "dve", "pool", "sp")


NQS = 8
NQG = 6


class Prog:

    def __init__(self, nc, es):
        self.nc = nc
        self.sems = {}
        names = list(ENGS) + [f"qs{i}" for i in range(NQS)] + [f"qg{i}" for i in range(NQG)]
        for name in names:
            self.sems[name] = es.enter_context(nc.semaphore("S_" + name))
        self.cnt = {k: 0 for k in self.sems}
        self.streams = {e: [] for e in ENGS}
        self.wm = {e: {k: 0 for k in self.sems} for e in ENGS}
        self.nq = {"ld": 0, "st": 0}
        self.epoch = []

    def buf(self):
        return {"w": None, "r": list(self.epoch) if KEPOCH else []}

    def bufs(self, n):
        return [self.buf() for _ in range(n)]

    def op(self, eng, fn, reads=(), writes=(), sem=None, inc=None):
        sem = sem or eng
        inc = inc if inc is not None else (16 if sem[0] == "q" else 1)
        waits = {}
        toks = []
        for b in reads:
            if b["w"] is not None:
                toks.append(b["w"])
        for b in writes:
            if b["w"] is not None:
                toks.append(b["w"])
            toks.extend(b["r"])
        for (s, v) in toks:
            if self.wm[eng][s] >= v:
                continue
            self.wm[eng][s] = v
            waits[s] = max(waits.get(s, 0), v)
        self.cnt[sem] += inc
        tok = (sem, self.cnt[sem])
        semh = self.sems[sem]
        sems = self.sems
        witems = tuple(waits.items())

        def run(e, fn=fn, witems=witems, semh=semh, inc=inc, sems=sems):
            for s, v in witems:
                e.wait_ge(sems[s], v)
            fn(e).then_inc(semh, inc)

        self.streams[eng].append(run)
        for b in writes:
            b["w"] = tok
            b["r"] = []
        for b in reads:
            b["r"].append(tok)
        return tok

    def _q(self, eng, fn, reads, writes, kind, nsem, pfx):
        i = self.nq[kind]
        self.nq[kind] = i + 1
        sem = f"{pfx}{i % nsem}"
        gate = 16 * (i // nsem)
        if gate > 0 and self.wm[eng][sem] < gate:
            self.wm[eng][sem] = gate
            h = self.sems[sem]
            self.streams[eng].append(lambda e, h=h, g=gate: e.wait_ge(h, g))
        return self.op(eng, fn, reads, writes, sem=sem)

    def ld(self, fn, reads=(), writes=()):
        return self._q("sp", fn, reads, writes, "ld", NQS, "qs")

    def st(self, fn, reads=(), writes=()):
        return self._q("pool", fn, reads, writes, "st", NQG, "qg")

    def barrier(self):
        snap = dict(self.cnt)
        sems = self.sems
        for e in ENGS:
            pend = []
            for s, v in snap.items():
                if v > 0 and s != e and self.wm[e][s] < v:
                    self.wm[e][s] = v
                    pend.append((s, v))
            if pend:
                def run(eng, pend=tuple(pend), sems=sems):
                    for s, v in pend:
                        eng.wait_ge(sems[s], v)
                self.streams[e].append(run)
        self.epoch = [(s, v) for s, v in snap.items() if v > 0]

    def finish(self, es):
        block = es.enter_context(self.nc.Block())
        streams = self.streams

        @block.tensor
        def _(e):
            for f in streams["pe"]:
                f(e)

        @block.scalar
        def _(e):
            for f in streams["act"]:
                f(e)

        @block.vector
        def _(e):
            for f in streams["dve"]:
                f(e)

        @block.gpsimd
        def _(e):
            for f in streams["pool"]:
                f(e)

        @block.sync
        def _(e):
            for f in streams["sp"]:
                f(e)


def ap(t, off, pattern):
    return bass.AP(t, off, [list(p) for p in pattern])


def build_program(kind="full", dbg=False):
    nc = bass.Bass("TRN2", target_bir_lowering=False, dynamic_dma_scratch_size=16384)
    es = ExitStack()
    io = {}
    io_kinds = {}
    io["_kinds"] = io_kinds

    def din(name, shape, dt=F32):
        io[name] = nc.dram_tensor(name, shape, dt, kind="ExternalInput")
        io_kinds[name] = "in"
        return io[name]

    def dout(name, shape, dt=F32):
        io[name] = nc.dram_tensor(name, shape, dt, kind="ExternalOutput")
        io_kinds[name] = "out"
        return io[name]

    def dscr(name, shape, dt=F32):
        k = "ExternalOutput" if dbg else "Internal"
        io[name] = nc.dram_tensor(name, shape, dt, kind=k)
        io_kinds[name] = "scr"
        return io[name]

    need_l0 = kind in ("l0", "full")
    need_d = kind in ("d12", "l1", "full")
    need_e1 = kind in ("e1", "l1", "full")
    need_e2 = kind in ("e2", "l1", "full")
    need_l1 = need_d or need_e1 or need_e2

    def dio(name, shape, dt, prod, cons):
        if kind == "full" or (kind == "l1" and prod != "l0"):
            return dscr(name, shape, dt)
        if prod == kind or (prod == "l0" and kind == "l0"):
            return dout(name, shape, dt)
        if kind in cons:
            return din(name, shape, dt)
        return dscr(name, shape, dt)

    if need_l0:
        xfm = din("xfm", [NB, DC, P, T])
        wrT = din("wrT", [D, D]); wkT = din("wkT", [D, D])
        wvT = din("wvT", [D, D]); woT = din("woT", [D, D])
        mixr_d = din("mixr", [P, DC]); mixk_d = din("mixk", [P, DC]); mixv_d = din("mixv", [P, DC])
        td_d = din("td", [P, DC]); tf_d = din("tf", [P, DC])
        gnw_d = din("gnw", [P, DC]); gnb_d = din("gnb", [P, DC])
        w1T = din("w1T", [D, HID]); w2T = din("w2T", [HID, D])
        cwl0_d = din("cwl0", [3, P, HC]); cbl0_d = din("cbl0", [P, HC])
    if need_d:
        ipwT_d = din("ipwT", [D, DIP]); opwT_d = din("opwT", [DIN, D])
        cwl1_d = din("cwl1", [4, P, CONVC]); cbl1_d = din("cbl1", [P, CONVC])
        dtb_d = din("dtb", [NH, 1]); alog_d = din("alog", [NH, 1]); dd_d = din("dd", [1, NH])
        nw_d = din("nw", [P, DINC])
    if need_e1:
        lng_d = din("lng", [P, DC]); lnb_d = din("lnb", [P, DC])
        fc1q_d = din("fc1q", [D, HID], I32); fc1s_d = din("fc1s", [DC, HID])
        fc1b_d = din("fc1b", [P, HC])
    if need_e2:
        fc2q_d = din("fc2q", [HID, D], I32); fc2s_d = din("fc2s", [HC, D])
        fc2b_d = din("fc2b", [P, DC])
    ident_d = din("ident", [P, P])
    ind2_d = din("ind2", [P, 2]); ind2T_d = din("ind2T", [2, P])

    if kind == "l0":
        x1_scr = dout("x1", [NB, DC, P, T])
    elif kind in ("l1", "d12"):
        x1_scr = din("x1", [NB, DC, P, T])
    elif kind == "full":
        x1_scr = dscr("x1", [NB, DC, P, T])
    else:
        x1_scr = None
    if need_e2:
        x2_out = dout("x2", [NB, DC, P, T])

    if need_l0:
        wq_scr = [nc.dram_tensor(f"wq{i}", [D, D], BF16, kind="Internal") for i in range(4)]
        w1q_scr = nc.dram_tensor("w1q", [D, HID], BF16, kind="Internal")
        w2q_scr = nc.dram_tensor("w2q", [HID, D], BF16, kind="Internal")
        n1_scr = dscr("n1s", [NB, DC, P, T], BF16)
        out_scr = dscr("outs", [NB, DC, P, T], BF16)
        hs_scr = dscr("hss", [NB, HC, P, T], BF16)
        st_m1 = nc.dram_tensor("st_m1", [1, 4], F32, kind="Internal")
        st_s1 = nc.dram_tensor("st_s1", [NB * NTL, 8, P], F32, kind="Internal")
        st_s2 = nc.dram_tensor("st_s2", [NB, 64, P], F32, kind="Internal")
    if need_d:
        ipw_scr = nc.dram_tensor("ipwq", [D, DIP], BF16, kind="Internal")
        opw_scr = nc.dram_tensor("opwq", [DIN, D], BF16, kind="Internal")
        sz_scr = dscr("szs", [NB, DINC, P, T], BF16)
        xs_scr = dscr("xss", [NB, DINC, P, T], BF16)
        bc_scr = dscr("bcs", [NB, 2, P, T], BF16)
        st_da = nc.dram_tensor("st_da", [NB * NTL, NH, TILE], F32, kind="Internal")
        st_dt = nc.dram_tensor("st_dt", [NB * NTL, NH, TILE], F32, kind="Internal")
        st_ea = nc.dram_tensor("st_ea", [NB * NTL, NH, TILE], F32, kind="Internal")
        st_cd = nc.dram_tensor("st_cd", [NB * NTL * NCC, NH], F32, kind="Internal")
    if need_e1:
        fc1d_scr = nc.dram_tensor("fc1d", [D, HID], BF16, kind="Internal")
    if need_e2:
        fc2d_scr = nc.dram_tensor("fc2d", [HID, D], BF16, kind="Internal")
    if need_l1:
        xn1_scr = dio("xn1s", [NB, DC, P, T], BF16, "d12", ("e1", "e2"))
        px2_scr = dio("px2s", [NB, DC, P, T], F32, "d12", ("e2",))
        hm_scr = dio("hms", [NB, HC, P, T], BF16, "e1", ("e2",))
    st_ln = nc.dram_tensor("st_ln", [NB * NTL, 1024], F32, kind="Internal")
    if dbg:
        dbg_bc = dout("dbg_bc", [NB * NTL, P, 1024])
        dbg_xn = dout("dbg_xn", [NB * NTL, P, TILE])
        dbg_s1 = dout("dbg_s1", [NB * NTL, P, 1024])
        dbg_am = dout("dbg_am", [NB * NTL, P, TILE])
        dbg_amn = dout("dbg_amn", [NB * NTL, P, TILE])
        dbg_scl = dout("dbg_scl", [NB * NTL, P, 12])
        dbg_bcD = dout("dbg_bcD", [NB * NTL, P, 1024])
        dbg_x1t = dout("dbg_x1t", [NB * NTL, P, TILE])
        dbg_rowtD = dout("dbg_rowtD", [NB * NTL, 1, 2048])
        dbg_ones = dout("dbg_ones", [NB * NTL, P, 1])
        dbg_ones2 = dout("dbg_ones2", [NB * NTL, P, 1])
    st_ln2 = nc.dram_tensor("st_ln2", [NB * NTL, 1024], F32, kind="Internal")
    st_rms = nc.dram_tensor("st_rms", [NB * NTL, TILE], F32, kind="Internal")

    pr = Prog(nc, es)
    B = pr.buf

    dbufs = {}

    def dbuf(t):
        if id(t) not in dbufs:
            dbufs[id(t)] = B()
        return dbufs[id(t)]

    sbo = lambda n, s, d=F32: es.enter_context(nc.sbuf_tensor(n, s, d))
    PSA = es.enter_context(nc.psum_tensor("PSA", [P, 512], F32))
    PSB = es.enter_context(nc.psum_tensor("PSB", [P, 512], F32))
    PS1 = es.enter_context(nc.psum_tensor("PS1", [P, 1024], F32))
    PS2 = es.enter_context(nc.psum_tensor("PS2", [P, 2048], F32))
    bPSA, bPSB = B(), B()
    bPS1 = pr.bufs(2)
    bPS2 = pr.bufs(4)

    ident_f = sbo("ident_f", [P, P]); b_identf = B()
    ident_b = sbo("ident_b", [P, P], BF16); b_identb = B()
    ones_f = sbo("ones_f", [P, 1]); b_onesf = B()
    ones_b = sbo("ones_b", [P, 1], BF16); b_onesb = B()
    ind2_sb = sbo("ind2_sb", [P, 2]); b_ind2 = B()
    ind2T_sb = sbo("ind2T_sb", [2, P]); b_ind2T = B()
    gtiny = sbo("gtiny", [2, 4 * DC]); b_gtiny = B()
    gtmp128 = sbo("gtmp128", [P, DC]); b_gtmp = B()
    rowt_sh = sbo("rowt_sh", [1, 2048]); b_rowt_sh = B()
    bc_sh = sbo("bc_sh", [P, 1024]); b_bc_sh = B()

    CVALS = (EPS, 1e-9, -1.0, -0.5, 1.0, 0.5, 1.0 / D, 1.0 / 127.0,
             1.0 / float(T * 64), 1.0 / float(DIN), 2.0)
    cpool = sbo("cpool", [P, len(CVALS)])
    b_cpool = B()
    for i, v in enumerate(CVALS):
        pr.op("pool", lambda e, i=i, v=float(v): e.memset(cpool[:, i:i + 1], v), writes=(b_cpool,))
        nc.const_aps.register(F32, float(v), cpool[:, i:i + 1])
    pr.barrier()

    if need_l0:
        mixr = sbo("mixr_s", [P, DC]); mixk = sbo("mixk_s", [P, DC]); mixv = sbo("mixv_s", [P, DC])
        wdec = sbo("wdec", [P, DC]); eu = sbo("eu", [P, DC])
        gnw = sbo("gnw_s", [P, DC]); gnb = sbo("gnb_s", [P, DC])
        cwm = [sbo(f"cwm{i}", [P, HC]) for i in range(3)]
        cb0 = sbo("cb0", [P, HC])
        m12 = sbo("m12", [P, 4])
        App = sbo("App", [P, 2 * DC]); Bpp = sbo("Bpp", [P, 2 * DC])
        gnsum = sbo("gnsum", [P, 2 * DC]); gnsq = sbo("gnsq", [P, 2 * DC])
        b_mixr, b_mixk, b_mixv = B(), B(), B()
        b_wdec, b_eu, b_gnw, b_gnb = B(), B(), B(), B()
        b_cwm = pr.bufs(3); b_cb0 = B(); b_m12 = B()
        b_App, b_Bpp, b_gnsum, b_gnsq = B(), B(), B(), B()
    if need_l1:
        cw1 = [sbo(f"cw1_{i}", [P, CONVC]) for i in range(4)]
        cb1 = sbo("cb1", [P, CONVC])
        dtb = sbo("dtb_s", [NH, 1]); negA = sbo("negA", [NH, 1])
        Dbc = sbo("Dbc", [P, NH])
        nw = sbo("nw_s", [P, DINC])
        lng = sbo("lng_s", [P, DC]); lnb = sbo("lnb_s", [P, DC])
        fc1b = sbo("fc1b_s", [P, HC]); fc2b05 = sbo("fc2b05", [P, DC])
        b_cw1 = pr.bufs(4); b_cb1 = B(); b_dtb = B(); b_negA = B()
        b_Dbc = B(); b_nw = B(); b_lng = B(); b_lnb = B(); b_fc1b = B(); b_fc2b = B()

    def simple_load(dst, src, wb):
        pr.ld(lambda e, d=dst, s=src: e.dma_start(d, s), reads=(), writes=(wb,))

    def bounce_bcast(row_ap, rowbuf, slot_dram, slot_off, ncols, bc_t, bcbuf):
        sb_ = dbuf(slot_dram)
        pr.st(lambda e, d=ap(slot_dram, slot_off, [[0, 1], [1, ncols]]), s=row_ap:
              e.dma_start(d, s), reads=(rowbuf,), writes=(sb_,))
        pr.ld(lambda e, d=bc_t[:, 0:ncols], s=ap(slot_dram, slot_off, [[0, P], [1, ncols]]):
              e.dma_start(d, s), reads=(sb_,), writes=(bcbuf,))

    simple_load(ident_f[:, :], ident_d[:, :], b_identf)
    simple_load(ind2_sb[:, :], ind2_d[:, :], b_ind2)
    simple_load(ind2T_sb[:, :], ind2T_d[:, :], b_ind2T)
    pr.op("dve", lambda e: e.tensor_copy(ident_b[:], ident_f[:]),
          reads=(b_identf,), writes=(b_identb,))
    pr.op("pool", lambda e: e.memset(ones_f[:], 1.0), writes=(b_onesf,))
    pr.op("dve", lambda e: e.tensor_copy(ones_b[:], ones_f[:]), reads=(b_onesf,), writes=(b_onesb,))

    esP = ExitStack()
    sbp = lambda n, s_, d=F32: esP.enter_context(nc.sbuf_tensor(n, s_, d))
    if need_l0:
        for t_, src, bb in ((mixr, mixr_d, b_mixr), (mixk, mixk_d, b_mixk), (mixv, mixv_d, b_mixv),
                            (gnw, gnw_d, b_gnw), (gnb, gnb_d, b_gnb), (cb0, cbl0_d, b_cb0)):
            simple_load(t_[:, :], src[:, :], bb)
        tdt = sbp("tdt", [P, DC]); b_tdt = B()
        simple_load(tdt[:, :], td_d[:, :], b_tdt)
        pr.op("act", lambda e: e.activation(wdec[:], tdt[:], AF.Exp), reads=(b_tdt,), writes=(b_wdec,))
        pr.op("act", lambda e: e.activation(wdec[:], wdec[:], AF.Exp, scale=-1.0),
              reads=(b_wdec,), writes=(b_wdec,))
        simple_load(tdt[:, :], tf_d[:, :], b_tdt)
        pr.op("act", lambda e: e.activation(eu[:], tdt[:], AF.Exp), reads=(b_tdt,), writes=(b_eu,))

        heavy_prep = (KPREP == 0)
        wtmp = sbp("wtmp", [P, D]); wbt = sbp("wbt", [P, D], BF16)
        b_wtmp, b_wbt = B(), B()
        for wi, wsrc in enumerate((wrT, wkT, wvT, woT) if (heavy_prep and not (KPBITS & 1)) else ()):
            for kc in range(DC):
                simple_load(wtmp[:, :], wsrc[kc * P:(kc + 1) * P, :], b_wtmp)
                pr.op("dve", lambda e: e.tensor_copy(wbt[:], wtmp[:]), reads=(b_wtmp,), writes=(b_wbt,))
                pr.st(lambda e, d=wq_scr[wi][kc * P:(kc + 1) * P, :]: e.dma_start(d, wbt[:]),
                      reads=(b_wbt,), writes=(dbuf(wq_scr[wi]),))

        wt4 = sbp("wt4", [P, HID]); wq4 = sbp("wq4", [P, HID], BF16)
        asum = sbp("asum", [P, HC]); mrow = sbp("mrow", [1, 16])
        b_wt4, b_wq4, b_asum, b_mrow = B(), B(), B(), B()
        for idx, (wsrc, nchunk, width) in enumerate(((w1T, DC, HID), (w2T, HC, D)) if (heavy_prep and not (KPBITS & 2)) else ()):
            for kc in range(nchunk):
                simple_load(wt4[:, 0:width], wsrc[kc * P:(kc + 1) * P, :], b_wt4)
                pr.op("dve", lambda e, kc=kc, width=width: e.reduce_sum(
                    asum[:, kc:kc + 1], wt4[:, 0:width], AX.X, apply_absolute_value=True),
                    reads=(b_wt4,), writes=(b_asum,))
            pr.op("pe", lambda e, nchunk=nchunk: e.matmul(
                PSA[0:1, 0:nchunk], ones_f[:, 0:1], asum[:, 0:nchunk], start=True, stop=True),
                reads=(b_asum, b_onesf), writes=(bPSA,))
            pr.op("dve", lambda e, nchunk=nchunk, idx=idx: e.reduce_sum(
                mrow[0:1, idx:idx + 1], PSA[0:1, 0:nchunk], AX.X), reads=(bPSA,), writes=(b_mrow,))
            nelem = float(width * nchunk * P)
            pr.op("dve", lambda e, idx=idx, nelem=nelem: e.tensor_scalar(
                mrow[0:1, 4 + idx:5 + idx], mrow[0:1, idx:idx + 1], 1.0 / nelem, 1e-5,
                ALU.mult, ALU.max), reads=(b_mrow,), writes=(b_mrow,))
            pr.op("dve", lambda e, idx=idx: e.reciprocal(
                mrow[0:1, 8 + idx:9 + idx], mrow[0:1, 4 + idx:5 + idx]),
                reads=(b_mrow,), writes=(b_mrow,))
        for dstc, srcc in (((12, 4), (13, 8), (14, 5), (15, 9)) if (heavy_prep and not (KPBITS & 2)) else ()):
            pr.op("dve", lambda e, dstc=dstc, srcc=srcc: e.tensor_copy(
                mrow[0:1, dstc:dstc + 1], mrow[0:1, srcc:srcc + 1]), reads=(b_mrow,), writes=(b_mrow,))
        if heavy_prep and not (KPBITS & 2):
            bounce_bcast(mrow[0:1, 12:16], b_mrow, st_m1, 0, 4, m12, b_m12)
        for (wsrc, nchunk, width, qdst, rmc) in (((w1T, DC, HID, w1q_scr, 1), (w2T, HC, D, w2q_scr, 3)) if (heavy_prep and not (KPBITS & 4)) else ()):
            for kc in range(nchunk):
                simple_load(wt4[:, 0:width], wsrc[kc * P:(kc + 1) * P, :], b_wt4)
                pr.op("dve", lambda e, width=width, rmc=rmc: e.tensor_scalar(
                    wt4[:, 0:width], wt4[:, 0:width], m12[:, rmc:rmc + 1], RC, ALU.mult, ALU.add),
                    reads=(b_m12,), writes=(b_wt4,))
                pr.op("dve", lambda e, width=width: e.tensor_scalar(
                    wt4[:, 0:width], wt4[:, 0:width], RC, 1.0, ALU.subtract, ALU.min),
                    reads=(), writes=(b_wt4,))
                pr.op("dve", lambda e, width=width: e.tensor_scalar(
                    wq4[:, 0:width], wt4[:, 0:width], -1.0, None, ALU.max),
                    reads=(b_wt4,), writes=(b_wq4,))
                pr.st(lambda e, d=qdst[kc * P:(kc + 1) * P, :], s=wq4[:, 0:width]:
                      e.dma_start(d, s), reads=(b_wq4,), writes=(dbuf(qdst),))
        cwt = sbp("cwt", [P, HC]); b_cwt = B()
        for i in (range(3) if (heavy_prep and not (KPBITS & 8)) else ()):
            simple_load(cwt[:, :], cwl0_d[i, :, :], b_cwt)
            pr.op("dve", lambda e, i=i: e.tensor_scalar(
                cwm[i][:], cwt[:], m12[:, 0:1], None, ALU.mult),
                reads=(b_cwt, b_m12), writes=(b_cwm[i],))

    if need_d:
        for t_, src, bb in ((cb1, cbl1_d, b_cb1), (dtb, dtb_d, b_dtb), (nw, nw_d, b_nw)):
            simple_load(t_[:, :], src[:, :], bb)
        for i in range(4):
            simple_load(cw1[i][:, :], cwl1_d[i, :, :], b_cw1[i])
    if need_e1:
        for t_, src, bb in ((lng, lng_d, b_lng), (lnb, lnb_d, b_lnb), (fc1b, fc1b_d, b_fc1b)):
            simple_load(t_[:, :], src[:, :], bb)
    if need_d:
        alt = sbp("alt", [NH, 1]); b_alt = B()
        simple_load(alt[:, :], alog_d[:, :], b_alt)
        pr.op("act", lambda e: e.activation(negA[:], alt[:], AF.Exp), reads=(b_alt,), writes=(b_negA,))
        pr.op("dve", lambda e: e.tensor_scalar(negA[:], negA[:], -1.0, None, ALU.mult),
              reads=(), writes=(b_negA,))
        if not int(os.environ.get("KNODBC", "0")):
            pr.ld(lambda e: e.dma_start(Dbc[:, :], ap(dd_d, 0, [[0, P], [1, NH]])), writes=(b_Dbc,))
        ipt = sbp("ipt", [P, DIP]); ipb = sbp("ipb", [P, DIP], BF16)
        b_ipt, b_ipb = B(), B()
        for kc in (range(DC) if not int(os.environ.get("KNOCAST", "0")) else ()):
            simple_load(ipt[:, :], ipwT_d[kc * P:(kc + 1) * P, :], b_ipt)
            pr.op("dve", lambda e: e.tensor_copy(ipb[:], ipt[:]), reads=(b_ipt,), writes=(b_ipb,))
            pr.st(lambda e, d=ipw_scr[kc * P:(kc + 1) * P, :]: e.dma_start(d, ipb[:]),
                  reads=(b_ipb,), writes=(dbuf(ipw_scr),))
        for kc in (range(DINC) if not int(os.environ.get("KNOCAST", "0")) else ()):
            simple_load(ipt[:, 0:D], opwT_d[kc * P:(kc + 1) * P, :], b_ipt)
            pr.op("dve", lambda e: e.tensor_copy(ipb[:, 0:D], ipt[:, 0:D]), reads=(b_ipt,), writes=(b_ipb,))
            pr.st(lambda e, d=opw_scr[kc * P:(kc + 1) * P, :]: e.dma_start(d, ipb[:, 0:D]),
                  reads=(b_ipb,), writes=(dbuf(opw_scr),))
    if need_e1 or need_e2:
        qrow = sbp("qrow", [P, HID], I32); sbc = sbp("sbc", [P, HID])
        qbf = sbp("qbf", [P, HID], BF16)
        b_qrow, b_sbc, b_qbf = B(), B(), B()
        fcjobs = []
        if need_e1:
            fcjobs.append((fc1q_d, fc1s_d, DC, HID, fc1d_scr))
        if need_e2:
            fcjobs.append((fc2q_d, fc2s_d, HC, D, fc2d_scr))
        for (qsrc, ssrc, nchunk, width, qdst) in fcjobs:
            for kc in range(nchunk):
                simple_load(qrow[:, 0:width], qsrc[kc * P:(kc + 1) * P, :], b_qrow)
                pr.ld(lambda e, s=ap(ssrc, kc * width, [[0, P], [1, width]]), w=width:
                      e.dma_start(sbc[:, 0:w], s), reads=(dbuf(ssrc),), writes=(b_sbc,))
                pr.op("dve", lambda e, width=width: e.scalar_tensor_tensor(
                    qbf[:, 0:width], qrow[:, 0:width], -128.0, sbc[:, 0:width],
                    ALU.add, ALU.mult), reads=(b_qrow, b_sbc), writes=(b_qbf,))
                pr.st(lambda e, d=qdst[kc * P:(kc + 1) * P, :], s=qbf[:, 0:width]:
                      e.dma_start(d, s), reads=(b_qbf,), writes=(dbuf(qdst),))
    if need_e2:
        fbt = sbp("fbt", [P, DC]); b_fbt = B()
        simple_load(fbt[:, :], fc2b_d[:, :], b_fbt)
        pr.op("dve", lambda e: e.tensor_scalar(fc2b05[:], fbt[:], 0.5, None, ALU.mult),
              reads=(b_fbt,), writes=(b_fc2b,))

    pr.barrier()
    esP.close()

    def emit_ln(xt, b_xt, sq, b_sq, rowt, b_rowt, bc, b_bc, slot_dram, slot):
        for c in range(DC):
            pr.op("pe", lambda e, c=c: e.matmul(
                PSA[0:1, 0:TILE], ones_f[:, 0:1], xt[:, c * TILE:(c + 1) * TILE],
                start=(c == 0), stop=(c == DC - 1)),
                reads=(b_xt, b_onesf), writes=(bPSA,))
        for c in range(DC):
            r = c % 2
            pr.op("act", lambda e, c=c, r=r: e.activation(
                sq[r][:], xt[:, c * TILE:(c + 1) * TILE], AF.Square),
                reads=(b_xt,), writes=(b_sq[r],))
            pr.op("pe", lambda e, r=r, c=c: e.matmul(
                PSB[0:1, 0:TILE], ones_f[:, 0:1], sq[r][:],
                start=(c == 0), stop=(c == DC - 1)),
                reads=(b_sq[r], b_onesf), writes=(bPSB,))
        pr.op("act", lambda e: e.activation(rowt[0:1, 0:TILE], PSA[0:1, 0:TILE], AF.Copy, scale=1.0 / D),
              reads=(bPSA,), writes=(b_rowt,))
        pr.op("act", lambda e: e.activation(rowt[0:1, TILE:2 * TILE], PSB[0:1, 0:TILE], AF.Copy, scale=1.0 / D),
              reads=(bPSB,), writes=(b_rowt,))
        pr.op("dve", lambda e: e.tensor_tensor(
            rowt[0:1, 2 * TILE:3 * TILE], rowt[0:1, 0:TILE], rowt[0:1, 0:TILE], ALU.mult),
            reads=(), writes=(b_rowt,))
        pr.op("dve", lambda e: e.tensor_tensor(
            rowt[0:1, 3 * TILE:4 * TILE], rowt[0:1, TILE:2 * TILE], rowt[0:1, 2 * TILE:3 * TILE],
            ALU.subtract), reads=(), writes=(b_rowt,))
        pr.op("act", lambda e: e.activation(
            rowt[0:1, 2 * TILE:3 * TILE], rowt[0:1, 3 * TILE:4 * TILE], AF.Ln, bias=EPS),
            reads=(), writes=(b_rowt,))
        pr.op("act", lambda e: e.activation(
            rowt[0:1, 2 * TILE:3 * TILE], rowt[0:1, 2 * TILE:3 * TILE], AF.Exp, scale=-0.5),
            reads=(), writes=(b_rowt,))
        pr.op("dve", lambda e: e.tensor_tensor(
            rowt[0:1, 3 * TILE:4 * TILE], rowt[0:1, 0:TILE], rowt[0:1, 2 * TILE:3 * TILE], ALU.mult),
            reads=(), writes=(b_rowt,))
        pr.op("pe", lambda e: e.matmul(
            PS2[:, 3 * TILE:3 * TILE + 512], ones_f[0:1, 0:1].broadcast_to([1, P]),
            rowt[0:1, 2 * TILE:3 * TILE], start=True, stop=True),
            reads=(b_rowt, b_onesf), writes=(bPS2[3],))
        pr.op("act", lambda e: e.activation(bc[:, 0:TILE], PS2[:, 3 * TILE:3 * TILE + 512], AF.Copy),
              reads=(bPS2[3],), writes=(b_bc,))
        pr.op("pe", lambda e: e.matmul(
            PS2[:, 3 * TILE:3 * TILE + 512], ones_f[0:1, 0:1].broadcast_to([1, P]),
            rowt[0:1, 3 * TILE:4 * TILE], start=True, stop=True),
            reads=(b_rowt, b_onesf), writes=(bPS2[3],))
        pr.op("act", lambda e: e.activation(bc[:, TILE:2 * TILE], PS2[:, 3 * TILE:3 * TILE + 512], AF.Copy),
              reads=(bPS2[3],), writes=(b_bc,))

    if need_l0:
      with ExitStack() as esA:
        sb = lambda n, s, d=F32: esA.enter_context(nc.sbuf_tensor(n, s, d))
        wr_bf = sb("wr_bf", [P, DC * D], BF16); wk_bf = sb("wk_bf", [P, DC * D], BF16)
        wv_bf = sb("wv_bf", [P, DC * D], BF16)
        b_wr, b_wk, b_wv = B(), B(), B()
        for wsb, wsrc, bb in ((wr_bf, wq_scr[0], b_wr), (wk_bf, wq_scr[1], b_wk), (wv_bf, wq_scr[2], b_wv)):
            for kc in range(DC):
                pr.ld(lambda e, d=wsb[:, kc * D:(kc + 1) * D], s=wsrc[kc * P:(kc + 1) * P, :]:
                      e.dma_start(d, s), reads=(dbuf(wsrc),), writes=(bb,))
        xt = [sb("xtA0", [P, DC * TILE])]; b_xtl = pr.bufs(1)
        xn = sb("xnA", [P, DC * TILE]); b_xn = B()
        sq = [sb(f"sqA{r}", [P, TILE]) for r in range(2)]; b_sql = pr.bufs(2)
        rowt = sb("rowtA", [1, 2048]); b_rowt = B()
        bc_ln = [sb("bclnA0", [P, 1024])]; b_bcln = pr.bufs(1)
        bc_s1 = [sb("bcs1A0", [P, 1024])]; b_bcs1 = pr.bufs(1)
        am = sb("amA", [P, TILE]); b_am = B()
        scl = sb("sclA", [P, 12]); b_scl = B()
        amn = sb("amnA", [P, TILE]); b_amn = B()
        rows8 = sb("rows8A", [8, P]); b_rows8 = B()
        xp = [sb(f"xpA{r}", [P, TILE]) for r in range(2)]; b_xpl = pr.bufs(2)
        dxn = [sb(f"dxnA{r}", [P, TILE]) for r in range(2)]; b_dxnl = pr.bufs(2)
        mixtmp = sb("mixtmpA", [P, TILE]); b_mixtmp = B()
        mixcarry = sb("mixcarryA", [P, DC]); b_mixc = B()
        xr_bf = sb("xrA", [P, DC * TILE], BF16); b_xr = B()
        xk_bf = sb("xkA", [P, DC * TILE], BF16); b_xk = B()
        xv_bf = sb("xvA", [P, DC * TILE], BF16); b_xv = B()
        q_c = [sb(f"qcA{r}", [P, TILE]) for r in range(2)]; b_qc = pr.bufs(2)
        xq_c = [sb(f"xqA{r}", [P, TILE], BF16) for r in range(2)]; b_xqc = pr.bufs(2)
        ek = [sb(f"ekA{r}", [P, TILE]) for r in range(2)]; b_ekl = pr.bufs(2)
        ekv = [sb(f"ekvA{r}", [P, TILE]) for r in range(2)]; b_ekvl = pr.bufs(2)
        rsig = [sb(f"rsigA{r}", [P, TILE]) for r in range(2)]; b_rsigl = pr.bufs(2)
        numer = [sb(f"numA{r}", [P, TILE]) for r in range(2)]; b_numl = pr.bufs(2)
        scanN = [sb(f"scanN{r}", [P, TILE + 1]) for r in range(2)]; b_scanN = pr.bufs(2)
        scanD = [sb(f"scanD{r}", [P, TILE + 1]) for r in range(2)]; b_scanD = pr.bufs(2)
        carryN = sb("carryN", [P, DC]); b_carryN = B()
        carryD = sb("carryD", [P, DC]); b_carryD = B()
        gns_t = sb("gns_t", [P, DC]); b_gns = B()
        gnq_t = sb("gnq_t", [P, DC]); b_gnq = B()
        out_bf = sb("outA", [P, DC * TILE], BF16); b_out = B()
        sqdead = sb("sqdeadA", [P, TILE]); b_sqd = B()

        for b in range(NB):
            pr.op("pool", lambda e: e.memset(mixcarry[:], 0.0), writes=(b_mixc,))
            pr.op("pool", lambda e: e.memset(carryN[:], 0.0), writes=(b_carryN,))
            pr.op("pool", lambda e: e.memset(carryD[:], 0.0), writes=(b_carryD,))
            pr.op("pool", lambda e, b=b: e.memset(gnsum[:, b * DC:(b + 1) * DC], 0.0), writes=(b_gnsum,))
            pr.op("pool", lambda e, b=b: e.memset(gnsq[:, b * DC:(b + 1) * DC], 0.0), writes=(b_gnsq,))
            for it in range(NTL):
                t0 = it * TILE
                slot = b * NTL + it
                r = 0
                xtr, b_xt = xt[0], b_xtl[0]
                bcl, b_bcl = bc_ln[0], b_bcln[0]
                bcs, b_bcs = bc_s1[0], b_bcs1[0]
                pr.ld(lambda e, b=b, t0=t0, d=xtr.rearrange("p (c t) -> p c t", c=DC):
                      e.dma_start(d, xfm[b, :, :, t0:t0 + TILE].rearrange("c p t -> p c t")),
                      reads=(dbuf(xfm),), writes=(b_xt,))
                emit_ln(xtr, b_xt, sq, b_sql, rowt, b_rowt, bcl, b_bcl, st_ln, slot)
                for c in (range(DC) if KTRUNC < 4 else ()):
                    pr.op("dve", lambda e, c=c, xtr=xtr, bcl=bcl: e.tensor_tensor(
                        xn[:, c * TILE:(c + 1) * TILE], xtr[:, c * TILE:(c + 1) * TILE],
                        bcl[:, 0:TILE], ALU.mult), reads=(b_xt, b_bcl), writes=(b_xn,))
                    pr.op("dve", lambda e, c=c, bcl=bcl: e.tensor_tensor(
                        xn[:, c * TILE:(c + 1) * TILE], xn[:, c * TILE:(c + 1) * TILE],
                        bcl[:, TILE:2 * TILE], ALU.subtract), reads=(b_bcl,), writes=(b_xn,))
                if dbg:
                    pr.st(lambda e, slot=slot, bcl=bcl: e.dma_start(io["dbg_bc"][slot], bcl[:, :]),
                          reads=(b_bcl,), writes=(dbuf(io["dbg_bc"]),))
                    pr.st(lambda e, slot=slot: e.dma_start(io["dbg_xn"][slot], xn[:, 0:TILE]),
                          reads=(b_xn,), writes=(dbuf(io["dbg_xn"]),))
                if KTRUNC >= 1:
                    continue
                pr.op("dve", lambda e: e.tensor_tensor(
                    am[:], xn[:, 0:TILE], xn[:, TILE:2 * TILE], ALU.max),
                    reads=(b_xn,), writes=(b_am,))
                for c in range(2, DC):
                    pr.op("dve", lambda e, c=c: e.tensor_tensor(
                        am[:], am[:], xn[:, c * TILE:(c + 1) * TILE], ALU.max),
                        reads=(b_xn,), writes=(b_am,))
                pr.op("dve", lambda e: e.tensor_tensor(
                    amn[:], xn[:, 0:TILE], xn[:, TILE:2 * TILE], ALU.min),
                    reads=(b_xn,), writes=(b_amn,))
                for c in range(2, DC):
                    pr.op("dve", lambda e, c=c: e.tensor_tensor(
                        amn[:], amn[:], xn[:, c * TILE:(c + 1) * TILE], ALU.min),
                        reads=(b_xn,), writes=(b_amn,))
                for j in range(4):
                    pr.op("pe", lambda e, j=j: e.transpose(
                        PS1[:, j * P:(j + 1) * P], am[:, j * P:(j + 1) * P], ident_f[:]),
                        reads=(b_am, b_identf), writes=(bPS1[0],))
                for j in range(4):
                    pr.op("pe", lambda e, j=j: e.transpose(
                        PS1[:, 512 + j * P:512 + (j + 1) * P], amn[:, j * P:(j + 1) * P], ident_f[:]),
                        reads=(b_amn, b_identf), writes=(bPS1[1],))
                for j in range(4):
                    pr.op("dve", lambda e, j=j: e.reduce_max(
                        scl[:, j:j + 1], PS1[:, j * P:(j + 1) * P], AX.X, apply_absolute_value=True),
                        reads=(bPS1[0],), writes=(b_scl,))
                for j in range(4):
                    pr.op("dve", lambda e, j=j: e.reduce_max(
                        scl[:, 8 + j:9 + j], PS1[:, 512 + j * P:512 + (j + 1) * P], AX.X,
                        apply_absolute_value=True),
                        reads=(bPS1[1],), writes=(b_scl,))
                if dbg:
                    pr.st(lambda e, slot=slot: e.dma_start(io["dbg_am"][slot], am[:, :]),
                          reads=(b_am,), writes=(dbuf(io["dbg_am"]),))
                    pr.st(lambda e, slot=slot: e.dma_start(io["dbg_amn"][slot], amn[:, :]),
                          reads=(b_amn,), writes=(dbuf(io["dbg_amn"]),))
                pr.op("dve", lambda e: e.tensor_tensor(
                    scl[:, 0:4], scl[:, 0:4], scl[:, 8:12], ALU.max), reads=(), writes=(b_scl,))
                pr.op("dve", lambda e: e.tensor_scalar(
                    scl[:, 0:4], scl[:, 0:4], 1e-5, None, ALU.max), reads=(), writes=(b_scl,))
                pr.op("dve", lambda e: e.tensor_scalar(
                    scl[:, 4:8], scl[:, 0:4], 1.0 / 127.0, None, ALU.mult), reads=(), writes=(b_scl,))
                pr.op("dve", lambda e: e.reciprocal(scl[:, 0:4], scl[:, 4:8]),
                      reads=(), writes=(b_scl,))
                if dbg:
                    pr.st(lambda e, slot=slot: e.dma_start(io["dbg_scl"][slot], scl[:, :]),
                          reads=(b_scl,), writes=(dbuf(io["dbg_scl"]),))
                pr.op("pe", lambda e: e.transpose(PSB[0:8, 0:P], scl[:, 0:8], ident_f[:]),
                      reads=(b_scl, b_identf), writes=(bPSB,))
                pr.op("act", lambda e: e.activation(rows8[:, :], PSB[0:8, 0:P], AF.Copy),
                      reads=(bPSB,), writes=(b_rows8,))
                sb_ = dbuf(st_s1)
                pr.st(lambda e, slot=slot, s=rows8[:, :]: e.dma_start(st_s1[slot, :, :], s),
                      reads=(b_rows8,), writes=(sb_,))
                pr.ld(lambda e, slot=slot, d=bcs: e.dma_start(
                    d[:, :], ap(st_s1, slot * 1024, [[0, P], [1, 1024]])),
                    reads=(sb_,), writes=(b_bcs,))
                if dbg:
                    pr.st(lambda e, slot=slot, bcs=bcs: e.dma_start(io["dbg_s1"][slot], bcs[:, :]),
                          reads=(b_bcs,), writes=(dbuf(io["dbg_s1"]),))
                for c in range(DC):
                    rc = c % 2
                    pr.op("pool", lambda e, c=c, rc=rc, bcs=bcs: e.tensor_tensor(
                        q_c[rc][:], xn[:, c * TILE:(c + 1) * TILE], bcs[:, 0:TILE], ALU.mult),
                        reads=(b_xn, b_bcs), writes=(b_qc[rc],))
                    pr.op("dve", lambda e, rc=rc: e.tensor_scalar(
                        q_c[rc][:], q_c[rc][:], RC, RC, ALU.add, ALU.subtract),
                        reads=(), writes=(b_qc[rc],))
                    pr.op("pool", lambda e, rc=rc, bcs=bcs: e.tensor_tensor(
                        xq_c[rc][:], q_c[rc][:], bcs[:, TILE:2 * TILE], ALU.mult),
                        reads=(b_qc[rc], b_bcs), writes=(b_xqc[rc],))
                    pr.st(lambda e, b=b, c=c, t0=t0, rc=rc: e.dma_start(
                        n1_scr[b, c, :, t0:t0 + TILE], xq_c[rc][:]),
                        reads=(b_xqc[rc],), writes=(dbuf(n1_scr),))
                if KTRUNC >= 2:
                    continue
                for c in range(DC):
                    rc = c % 2
                    xpc, b_xp = xp[rc], b_xpl[rc]
                    dxc, b_dx = dxn[rc], b_dxnl[rc]
                    pr.op("dve", lambda e, c=c, xpc=xpc: e.tensor_copy(
                        xpc[:, 1:TILE], xn[:, c * TILE:c * TILE + TILE - 1]),
                        reads=(b_xn,), writes=(b_xp,))
                    pr.op("dve", lambda e, c=c, xpc=xpc: e.tensor_copy(
                        xpc[:, 0:1], mixcarry[:, c:c + 1]), reads=(b_mixc,), writes=(b_xp,))
                    pr.op("dve", lambda e, c=c: e.tensor_copy(
                        mixcarry[:, c:c + 1], xn[:, c * TILE + TILE - 1:(c + 1) * TILE]),
                        reads=(b_xn,), writes=(b_mixc,))
                    pr.op("dve", lambda e, c=c, xpc=xpc, dxc=dxc: e.tensor_tensor(
                        dxc[:], xn[:, c * TILE:(c + 1) * TILE], xpc[:], ALU.subtract),
                        reads=(b_xn, b_xp), writes=(b_dx,))
                    pr.op("dve", lambda e, c=c, xpc=xpc, dxc=dxc: e.scalar_tensor_tensor(
                        xr_bf[:, c * TILE:(c + 1) * TILE], dxc[:], mixr[:, c:c + 1], xpc[:],
                        ALU.mult, ALU.add), reads=(b_dx, b_xp, b_mixr), writes=(b_xr,))
                    pr.op("pool", lambda e, c=c, dxc=dxc: e.tensor_scalar(
                        mixtmp[:], dxc[:], mixk[:, c:c + 1], None, ALU.mult),
                        reads=(b_dx, b_mixk), writes=(b_mixtmp,))
                    pr.op("pool", lambda e, c=c, xpc=xpc: e.tensor_tensor(
                        xk_bf[:, c * TILE:(c + 1) * TILE], mixtmp[:], xpc[:], ALU.add),
                        reads=(b_mixtmp, b_xp), writes=(b_xk,))
                    pr.op("dve", lambda e, c=c, xpc=xpc, dxc=dxc: e.scalar_tensor_tensor(
                        xv_bf[:, c * TILE:(c + 1) * TILE], dxc[:], mixv[:, c:c + 1], xpc[:],
                        ALU.mult, ALU.add), reads=(b_dx, b_xp, b_mixv), writes=(b_xv,))
                if KTRUNC >= 3:
                    continue
                for eo in range(DC):
                    if eo % 2 == 0:
                        aps = [(PS2[:, 0:TILE], bPS2[0]), (PS2[:, TILE:2 * TILE], bPS2[1]),
                               (PS2[:, 2 * TILE:3 * TILE], bPS2[2])]
                    else:
                        aps = [(PS2[:, 3 * TILE:4 * TILE], bPS2[3]), (PS1[:, 0:TILE], bPS1[0]),
                               (PS1[:, TILE:2 * TILE], bPS1[1])]
                    (psr, bpsr), (psk, bpsk), (psv, bpsv) = aps
                    for (psx, bpsx, wsb, bw, xsb, bx) in (
                            (psr, bpsr, wr_bf, b_wr, xr_bf, b_xr),
                            (psk, bpsk, wk_bf, b_wk, xk_bf, b_xk),
                            (psv, bpsv, wv_bf, b_wv, xv_bf, b_xv)):
                        def mmgroup(e, psx=psx, wsb=wsb, xsb=xsb, eo=eo):
                            for kc in range(DC):
                                mm = e.matmul(psx, wsb[:, kc * D + eo * P:kc * D + (eo + 1) * P],
                                              xsb[:, kc * TILE:(kc + 1) * TILE],
                                              start=(kc == 0), stop=(kc == DC - 1))
                            return mm
                        pr.op("pe", mmgroup, reads=(bw, bx), writes=(bpsx,))
                    re_ = eo % 2
                    pr.op("act", lambda e, re_=re_, psk=psk: e.activation(
                        ek[re_][:], psk, AF.Exp), reads=(bpsk,), writes=(b_ekl[re_],))
                    pr.op("act", lambda e, re_=re_, psr=psr: e.activation(
                        rsig[re_][:], psr, AF.Sigmoid), reads=(bpsr,), writes=(b_rsigl[re_],))
                    pr.op("dve", lambda e, re_=re_, psv=psv: e.tensor_tensor(
                        ekv[re_][:], ek[re_][:], psv, ALU.mult),
                        reads=(b_ekl[re_], bpsv), writes=(b_ekvl[re_],))
                    for (scn, b_scn, carr, b_carr, data, b_data) in (
                            (scanN[re_], b_scanN[re_], carryN, b_carryN, ekv[re_], b_ekvl[re_]),
                            (scanD[re_], b_scanD[re_], carryD, b_carryD, ek[re_], b_ekl[re_])):
                        pr.op("dve", lambda e, scn=scn, carr=carr, eo=eo: e.tensor_copy(
                            scn[:, 0:1], carr[:, eo:eo + 1]), reads=(b_carr,), writes=(b_scn,))
                        pr.op("dve", lambda e, scn=scn, carr=carr, data=data, eo=eo:
                              e.tensor_tensor_scan(
                                  scn[:, 1:TILE + 1], wdec[:, eo:eo + 1].broadcast_to([P, TILE]),
                                  data[:], carr[:, eo:eo + 1], ALU.mult, ALU.add),
                              reads=(b_carr, b_data, b_wdec), writes=(b_scn,))
                        pr.op("dve", lambda e, scn=scn, carr=carr, eo=eo: e.tensor_copy(
                            carr[:, eo:eo + 1], scn[:, TILE:TILE + 1]),
                            reads=(b_scn,), writes=(b_carr,))
                    pr.op("dve", lambda e, re_=re_, eo=eo: e.scalar_tensor_tensor(
                        numer[re_][:], ekv[re_][:], eu[:, eo:eo + 1], scanN[re_][:, 0:TILE],
                        ALU.mult, ALU.add),
                        reads=(b_ekvl[re_], b_scanN[re_], b_eu), writes=(b_numl[re_],))
                    pr.op("dve", lambda e, re_=re_, eo=eo: e.scalar_tensor_tensor(
                        ek[re_][:], ek[re_][:], eu[:, eo:eo + 1], scanD[re_][:, 0:TILE],
                        ALU.mult, ALU.add),
                        reads=(b_scanD[re_], b_eu), writes=(b_ekl[re_],))
                    pr.op("act", lambda e, re_=re_: e.activation(
                        ekv[re_][:], ek[re_][:], AF.Ln, bias=1e-9),
                        reads=(b_ekl[re_],), writes=(b_ekvl[re_],))
                    pr.op("act", lambda e, re_=re_: e.activation(
                        ek[re_][:], ekv[re_][:], AF.Exp, scale=-1.0),
                        reads=(b_ekvl[re_],), writes=(b_ekl[re_],))
                    pr.op("dve", lambda e, re_=re_: e.tensor_tensor(
                        numer[re_][:], numer[re_][:], ek[re_][:], ALU.mult),
                        reads=(b_ekl[re_],), writes=(b_numl[re_],))
                    pr.op("dve", lambda e, re_=re_, eo=eo: e.scalar_tensor_tensor(
                        out_bf[:, eo * TILE:(eo + 1) * TILE], numer[re_][:], 1.0, rsig[re_][:],
                        ALU.mult, ALU.mult, accum_out=gns_t[:, eo:eo + 1]),
                        reads=(b_numl[re_], b_rsigl[re_]), writes=(b_out, b_gns))
                    pr.op("act", lambda e, eo=eo: e.activation(
                        sqdead[:], out_bf[:, eo * TILE:(eo + 1) * TILE], AF.Square,
                        accum_out=gnq_t[:, eo:eo + 1]),
                        reads=(b_out,), writes=(b_sqd, b_gnq))
                pr.op("dve", lambda e, b=b: e.tensor_tensor(
                    gnsum[:, b * DC:(b + 1) * DC], gnsum[:, b * DC:(b + 1) * DC], gns_t[:], ALU.add),
                    reads=(b_gns,), writes=(b_gnsum,))
                pr.op("dve", lambda e, b=b: e.tensor_tensor(
                    gnsq[:, b * DC:(b + 1) * DC], gnsq[:, b * DC:(b + 1) * DC], gnq_t[:], ALU.add),
                    reads=(b_gnq,), writes=(b_gnsq,))
                pr.st(lambda e, b=b, t0=t0: e.dma_start(
                    out_scr[b, :, :, t0:t0 + TILE].rearrange("c p t -> p c t"),
                    out_bf.rearrange("p (c t) -> p c t", c=DC)),
                    reads=(b_out,), writes=(dbuf(out_scr),))
            NM = float(T * 64)
            pr.op("pe", lambda e, b=b: e.matmul(
                PSA[0:2, 0:DC], ind2_sb[:, :], gnsum[:, b * DC:(b + 1) * DC],
                start=True, stop=True), reads=(b_gnsum, b_ind2), writes=(bPSA,))
            pr.op("pe", lambda e, b=b: e.matmul(
                PSB[0:2, 0:DC], ind2_sb[:, :], gnsq[:, b * DC:(b + 1) * DC],
                start=True, stop=True), reads=(b_gnsq, b_ind2), writes=(bPSB,))
            pr.op("act", lambda e, NM=NM: e.activation(
                gtiny[0:2, 0:DC], PSA[0:2, 0:DC], AF.Copy, scale=1.0 / NM),
                reads=(bPSA,), writes=(b_gtiny,))
            pr.op("act", lambda e, NM=NM: e.activation(
                gtiny[0:2, DC:2 * DC], PSB[0:2, 0:DC], AF.Copy, scale=1.0 / NM),
                reads=(bPSB,), writes=(b_gtiny,))
            pr.op("dve", lambda e: e.tensor_tensor(
                gtiny[0:2, 2 * DC:3 * DC], gtiny[0:2, 0:DC], gtiny[0:2, 0:DC], ALU.mult),
                reads=(), writes=(b_gtiny,))
            pr.op("dve", lambda e: e.tensor_tensor(
                gtiny[0:2, 3 * DC:4 * DC], gtiny[0:2, DC:2 * DC], gtiny[0:2, 2 * DC:3 * DC],
                ALU.subtract), reads=(), writes=(b_gtiny,))
            pr.op("act", lambda e: e.activation(
                gtiny[0:2, 2 * DC:3 * DC], gtiny[0:2, 3 * DC:4 * DC], AF.Ln, bias=EPS),
                reads=(), writes=(b_gtiny,))
            pr.op("act", lambda e: e.activation(
                gtiny[0:2, 2 * DC:3 * DC], gtiny[0:2, 2 * DC:3 * DC], AF.Exp, scale=-0.5),
                reads=(), writes=(b_gtiny,))
            pr.op("pe", lambda e: e.matmul(
                PSA[:, 0:DC], ind2T_sb[:, :], gtiny[0:2, 2 * DC:3 * DC], start=True, stop=True),
                reads=(b_gtiny, b_ind2T), writes=(bPSA,))
            pr.op("pe", lambda e: e.matmul(
                PSB[:, 0:DC], ind2T_sb[:, :], gtiny[0:2, 0:DC], start=True, stop=True),
                reads=(b_gtiny, b_ind2T), writes=(bPSB,))
            pr.op("dve", lambda e, b=b: e.tensor_tensor(
                App[:, b * DC:(b + 1) * DC], PSA[:, 0:DC], gnw[:], ALU.mult),
                reads=(bPSA, b_gnw), writes=(b_App,))
            pr.op("dve", lambda e, b=b: e.tensor_tensor(
                gtmp128[:, 0:DC], PSB[:, 0:DC], App[:, b * DC:(b + 1) * DC], ALU.mult),
                reads=(bPSB, b_App), writes=(b_gtmp,))
            pr.op("dve", lambda e, b=b: e.tensor_tensor(
                Bpp[:, b * DC:(b + 1) * DC], gnb[:], gtmp128[:, 0:DC], ALU.subtract),
                reads=(b_gnb, b_gtmp), writes=(b_Bpp,))
      pr.barrier()

      with ExitStack() as esB:
       if KTRUNC >= 1:
        pass
       else:
        sb = lambda n, s, d=F32: esB.enter_context(nc.sbuf_tensor(n, s, d))
        n1sb = sb("n1B", [P, DC * T], BF16); b_n1 = B()
        w1t = [sb(f"w1B{r}", [P, DC * P], BF16) for r in range(2)]; b_w1t = pr.bufs(2)
        h_buf = sb("hB", [P, T + 2]); b_h = B()
        acc = sb("accB", [P, T]); b_acc = B()
        p1 = sb("p1B", [P, T]); b_p1 = B()
        hs_bf = [sb(f"hsB{r}", [P, T], BF16) for r in range(2)]; b_hs = pr.bufs(2)
        amax2 = sb("amax2B", [P, T]); b_am2 = B()
        amin2 = sb("amin2B", [P, T]); b_an2 = B()
        s2tmp = sb("s2tmpB", [P, 32]); b_s2t = B()
        s2cols = sb("s2colsB", [P, 64]); b_s2c = B()
        rows64 = sb("rows64B", [64, P]); b_r64 = B()
        pr.op("pool", lambda e: e.memset(h_buf[:, 0:1], 0.0), writes=(b_h,))
        pr.op("pool", lambda e: e.memset(h_buf[:, T + 1:T + 2], 0.0), writes=(b_h,))
        for b in range(NB):
            for c in range(DC):
                pr.ld(lambda e, b=b, c=c, d=n1sb[:, c * T:(c + 1) * T]:
                      e.dma_start(d, n1_scr[b, c, :, :]), reads=(dbuf(n1_scr),), writes=(b_n1,))
            pr.op("pool", lambda e: e.memset(amax2[:], 0.0), writes=(b_am2,))
            pr.op("pool", lambda e: e.memset(amin2[:], 0.0), writes=(b_an2,))
            for hb in range(HC):
                r = hb % 2
                pr.ld(lambda e, hb=hb, r=r, d=w1t[r].rearrange("p (k c) -> p k c", k=DC):
                      e.dma_start(d, w1q_scr[:, hb * P:(hb + 1) * P].rearrange("(k p) c -> p k c", p=P)),
                      reads=(dbuf(w1q_scr),), writes=(b_w1t[r],))
                for tj in range(NTL):
                    psx, bpsx = (PS2[:, 0:TILE], bPS2[0]) if tj % 2 == 0 else (PS2[:, TILE:2 * TILE], bPS2[1])
                    def mmg(e, psx=psx, r=r, tj=tj):
                        for kc in range(DC):
                            mm = e.matmul(psx, w1t[r][:, kc * P:(kc + 1) * P],
                                          n1sb[:, kc * T + tj * TILE:kc * T + (tj + 1) * TILE],
                                          start=(kc == 0), stop=(kc == DC - 1))
                        return mm
                    pr.op("pe", mmg, reads=(b_w1t[r], b_n1), writes=(bpsx,))
                    pr.op("act", lambda e, psx=psx, tj=tj: e.activation(
                        h_buf[:, 1 + tj * TILE:1 + (tj + 1) * TILE], psx, AF.Copy),
                        reads=(bpsx,), writes=(b_h,))
                pr.op("pool", lambda e, hb=hb: e.tensor_scalar(
                    p1[:], h_buf[:, 1:T + 1], cwm[1][:, hb:hb + 1], None, ALU.mult),
                    reads=(b_h, b_cwm[1]), writes=(b_p1,))
                pr.op("dve", lambda e, hb=hb: e.scalar_tensor_tensor(
                    acc[:], h_buf[:, 0:T], cwm[0][:, hb:hb + 1], p1[:], ALU.mult, ALU.add),
                    reads=(b_h, b_p1, b_cwm[0]), writes=(b_acc,))
                pr.op("dve", lambda e, hb=hb: e.scalar_tensor_tensor(
                    acc[:], h_buf[:, 2:T + 2], cwm[2][:, hb:hb + 1], acc[:], ALU.mult, ALU.add),
                    reads=(b_h, b_cwm[2]), writes=(b_acc,))
                pr.op("act", lambda e, hb=hb, r=r: e.activation(
                    hs_bf[r][:], acc[:], AF.Silu, bias=cb0[:, hb:hb + 1]),
                    reads=(b_acc, b_cb0), writes=(b_hs[r],))
                pr.op("dve", lambda e, r=r: e.tensor_tensor(
                    amax2[:], amax2[:], hs_bf[r][:], ALU.max),
                    reads=(b_hs[r],), writes=(b_am2,))
                pr.op("dve", lambda e, r=r: e.tensor_tensor(
                    amin2[:], amin2[:], hs_bf[r][:], ALU.min),
                    reads=(b_hs[r],), writes=(b_an2,))
                pr.st(lambda e, b=b, hb=hb, r=r: e.dma_start(hs_scr[b, hb, :, :], hs_bf[r][:]),
                      reads=(b_hs[r],), writes=(dbuf(hs_scr),))
            for src, bsrc, dstcols, bdst in ((amax2, b_am2, s2cols, b_s2c), (amin2, b_an2, s2tmp, b_s2t)):
                for g in range(4):
                    for jj in range(8):
                        j = g * 8 + jj
                        pr.op("pe", lambda e, j=j, jj=jj, src=src: e.transpose(
                            PS1[:, jj * P:(jj + 1) * P], src[:, j * P:(j + 1) * P], ident_f[:]),
                            reads=(bsrc, b_identf), writes=(bPS1[jj // 4],))
                    for jj in range(8):
                        j = g * 8 + jj
                        pr.op("dve", lambda e, j=j, jj=jj, dstcols=dstcols: e.reduce_max(
                            dstcols[:, j:j + 1], PS1[:, jj * P:(jj + 1) * P], AX.X,
                            apply_absolute_value=True),
                            reads=(bPS1[jj // 4],), writes=(bdst,))
            pr.op("dve", lambda e: e.tensor_tensor(
                s2cols[:, 0:32], s2cols[:, 0:32], s2tmp[:, 0:32], ALU.max),
                reads=(b_s2t,), writes=(b_s2c,))
            pr.op("dve", lambda e: e.tensor_scalar(
                s2cols[:, 0:32], s2cols[:, 0:32], 1e-5, None, ALU.max), reads=(), writes=(b_s2c,))
            pr.op("dve", lambda e: e.tensor_scalar(
                s2cols[:, 32:64], s2cols[:, 0:32], 1.0 / 127.0, None, ALU.mult),
                reads=(), writes=(b_s2c,))
            pr.op("dve", lambda e: e.reciprocal(s2cols[:, 0:32], s2cols[:, 32:64]),
                  reads=(), writes=(b_s2c,))
            pr.op("pe", lambda e: e.transpose(PS1[0:64, 0:P], s2cols[:, 0:64], ident_f[:]),
                  reads=(b_s2c, b_identf), writes=(bPS1[0],))
            pr.op("act", lambda e: e.activation(rows64[:, :], PS1[0:64, 0:P], AF.Copy),
                  reads=(bPS1[0],), writes=(b_r64,))
            pr.st(lambda e, b=b: e.dma_start(st_s2[b, :, :], rows64[:, :]),
                  reads=(b_r64,), writes=(dbuf(st_s2),))
      pr.barrier()

      with ExitStack() as esC:
       if KTRUNC >= 1:
        pass
       else:
        sb = lambda n, s, d=F32: esC.enter_context(nc.sbuf_tensor(n, s, d))
        w2sb = sb("w2C", [P, HC * D], BF16); b_w2 = B()
        wosb = sb("woC", [P, DC * D], BF16); b_wo = B()
        for kc in range(HC):
            pr.ld(lambda e, kc=kc, d=w2sb[:, kc * D:(kc + 1) * D]:
                  e.dma_start(d, w2q_scr[kc * P:(kc + 1) * P, :]), reads=(dbuf(w2q_scr),), writes=(b_w2,))
        for kc in range(DC):
            pr.ld(lambda e, kc=kc, d=wosb[:, kc * D:(kc + 1) * D]:
                  e.dma_start(d, wq_scr[3][kc * P:(kc + 1) * P, :]), reads=(dbuf(wq_scr[3]),), writes=(b_wo,))
        hs_t = sb("hsC", [P, HC * TILE], BF16); b_hst = B()
        out_t = sb("outC", [P, DC * TILE], BF16); b_outt = B()
        bc2 = [sb(f"bc2C{r}", [P, 1024]) for r in range(2)]; b_bc2 = pr.bufs(2)
        xq2 = sb("xq2C", [P, HC * TILE], BF16); b_xq2 = B()
        q2 = [sb(f"q2C{r}", [P, TILE]) for r in range(2)]; b_q2 = pr.bufs(2)
        y_t = sb("yC", [P, DC * TILE], BF16); b_yt = B()
        x_c = [sb(f"xcC{r}", [P, TILE]) for r in range(2)]; b_xc = pr.bufs(2)
        tz = [sb(f"tzC{r}", [P, TILE]) for r in range(2)]; b_tz = pr.bufs(2)
        tw = [sb(f"twC{r}", [P, TILE]) for r in range(2)]; b_tw = pr.bufs(2)
        x1c = [sb(f"x1C{r}", [P, TILE]) for r in range(2)]; b_x1c = pr.bufs(2)
        for b in range(NB):
            for it in range(NTL):
                t0 = it * TILE
                rb = it % 2
                bcx, b_bcx = bc2[rb], b_bc2[rb]
                pr.ld(lambda e, b=b, t0=t0, d=hs_t.rearrange("p (c t) -> p c t", c=HC):
                      e.dma_start(d, hs_scr[b, :, :, t0:t0 + TILE].rearrange("c p t -> p c t")),
                      reads=(dbuf(hs_scr),), writes=(b_hst,))
                pr.ld(lambda e, b=b, t0=t0, d=out_t.rearrange("p (c t) -> p c t", c=DC):
                      e.dma_start(d, out_scr[b, :, :, t0:t0 + TILE].rearrange("c p t -> p c t")),
                      reads=(dbuf(out_scr),), writes=(b_outt,))
                pr.ld(lambda e, b=b, it=it, d=bcx: e.dma_start(
                    d[:, :], ap(st_s2, b * 64 * P + it * 4 * P, [[0, P], [32 * P, 2], [1, TILE]])),
                    reads=(dbuf(st_s2),), writes=(b_bcx,))
                for kc in range(HC):
                    rc = kc % 2
                    pr.op("pool", lambda e, kc=kc, rc=rc, bcx=bcx: e.tensor_tensor(
                        q2[rc][:], hs_t[:, kc * TILE:(kc + 1) * TILE], bcx[:, 0:TILE], ALU.mult),
                        reads=(b_hst, b_bcx), writes=(b_q2[rc],))
                    pr.op("dve", lambda e, rc=rc: e.tensor_scalar(
                        q2[rc][:], q2[rc][:], RC, RC, ALU.add, ALU.subtract),
                        reads=(), writes=(b_q2[rc],))
                    pr.op("pool", lambda e, kc=kc, rc=rc, bcx=bcx: e.tensor_tensor(
                        xq2[:, kc * TILE:(kc + 1) * TILE], q2[rc][:], bcx[:, TILE:2 * TILE], ALU.mult),
                        reads=(b_q2[rc], b_bcx), writes=(b_xq2,))
                for c in range(DC):
                    pr.op("dve", lambda e, c=c, b=b: e.tensor_scalar(
                        y_t[:, c * TILE:(c + 1) * TILE], out_t[:, c * TILE:(c + 1) * TILE],
                        App[:, b * DC + c:b * DC + c + 1], Bpp[:, b * DC + c:b * DC + c + 1],
                        ALU.mult, ALU.add), reads=(b_outt, b_App, b_Bpp), writes=(b_yt,))
                for eo in range(DC):
                    rc = eo % 2
                    psz, bpsz = (PS2[:, 0:TILE], bPS2[0]) if rc == 0 else (PS2[:, 2 * TILE:3 * TILE], bPS2[2])
                    psw, bpsw = (PS2[:, TILE:2 * TILE], bPS2[1]) if rc == 0 else (PS2[:, 3 * TILE:4 * TILE], bPS2[3])
                    def mmz(e, psz=psz, eo=eo):
                        for kc in range(HC):
                            mm = e.matmul(psz, w2sb[:, kc * D + eo * P:kc * D + (eo + 1) * P],
                                          xq2[:, kc * TILE:(kc + 1) * TILE],
                                          start=(kc == 0), stop=(kc == HC - 1))
                        return mm
                    pr.op("pe", mmz, reads=(b_w2, b_xq2), writes=(bpsz,))
                    def mmw(e, psw=psw, eo=eo):
                        for kc in range(DC):
                            mm = e.matmul(psw, wosb[:, kc * D + eo * P:kc * D + (eo + 1) * P],
                                          y_t[:, kc * TILE:(kc + 1) * TILE],
                                          start=(kc == 0), stop=(kc == DC - 1))
                        return mm
                    pr.op("pe", mmw, reads=(b_wo, b_yt), writes=(bpsw,))
                    pr.ld(lambda e, b=b, eo=eo, t0=t0, rc=rc: e.dma_start(
                        x_c[rc][:], xfm[b, eo, :, t0:t0 + TILE]), reads=(dbuf(xfm),), writes=(b_xc[rc],))
                    pr.op("dve", lambda e, rc=rc, psz=psz: e.tensor_scalar(
                        tz[rc][:], psz, m12[:, 2:3], None, ALU.mult),
                        reads=(bpsz, b_m12), writes=(b_tz[rc],))
                    pr.op("dve", lambda e, rc=rc, psw=psw: e.scalar_tensor_tensor(
                        tw[rc][:], psw, 1.0, x_c[rc][:], ALU.mult, ALU.add),
                        reads=(bpsw, b_xc[rc]), writes=(b_tw[rc],))
                    pr.op("dve", lambda e, rc=rc: e.tensor_tensor(
                        x1c[rc][:], tz[rc][:], tw[rc][:], ALU.add),
                        reads=(b_tz[rc], b_tw[rc]), writes=(b_x1c[rc],))
                    pr.st(lambda e, b=b, eo=eo, t0=t0, rc=rc: e.dma_start(
                        x1_scr[b, eo, :, t0:t0 + TILE], x1c[rc][:]),
                        reads=(b_x1c[rc],), writes=(dbuf(x1_scr),))
      pr.barrier()

    if need_l1:
        env = dict(locals())
        emit_l1(env)

    pr.barrier()
    pr.finish(es)
    return nc, es, io


def emit_l1(env):
    g = type("G", (), env)
    _emit_l1_impl(g)



def _emit_l1_impl(g):
    nc, pr, io = g.nc, g.pr, g.io
    B = g.B
    dbuf = g.dbuf
    PSA, PSB, PS1, PS2 = g.PSA, g.PSB, g.PS1, g.PS2
    bPSA, bPSB, bPS1, bPS2 = g.bPSA, g.bPSB, g.bPS1, g.bPS2
    ident_f, b_identf = g.ident_f, g.b_identf
    ident_b, b_identb = g.ident_b, g.b_identb
    ones_f, b_onesf = g.ones_f, g.b_onesf
    ones_b, b_onesb = g.ones_b, g.b_onesb
    dbg = g.dbg

    ps1bf = [PS1[:, 0:512].bitcast(BF16), PS1[:, 512:1024].bitcast(BF16)]
    psabf = PSA[:, 0:512].bitcast(BF16)
    psbbf = PSB[:, 0:512].bitcast(BF16)

    if not g.need_d:
        pass
    with ExitStack() as esD:
     if not g.need_d:
        pass
     else:
        sb = lambda n, s_, d=F32: esD.enter_context(nc.sbuf_tensor(n, s_, d))
        ipw = sb("ipwD", [P, DC * DIP], BF16); b_ipw = B()
        for kc in range(DC):
            pr.ld(lambda e, kc=kc, d=ipw[:, kc * DIP:(kc + 1) * DIP]:
                  e.dma_start(d, g.ipw_scr[kc * P:(kc + 1) * P, :]),
                  reads=(dbuf(g.ipw_scr),), writes=(b_ipw,))
        x1t = sb("x1tD", [P, DC * TILE]); b_x1t = B()
        sq = [sb(f"sqD{r}", [P, TILE]) for r in range(2)]; b_sql = pr.bufs(2)
        rowt = g.rowt_sh; b_rowt = g.b_rowt_sh
        bc = g.bc_sh; b_bc = g.b_bc_sh
        xtmp = sb("xtmpD", [P, TILE]); b_xtmp = B()
        xn1 = sb("xn1D", [P, DC * TILE], BF16); b_xn1 = B()
        xbc = [sb(f"xbcD{r}", [P, TILE + 3]) for r in range(2)]; b_xbc = pr.bufs(2)
        cvt = [sb(f"cvtD{r}", [P, TILE]) for r in range(2)]; b_cvt = pr.bufs(2)
        so = [sb(f"soD{r}", [P, TILE], BF16) for r in range(2)]; b_so = pr.bufs(2)
        cvcar = sb("cvcarD", [P, CONVC * 3]); b_cvc = B()
        dtw = sb("dtwD", [NH, 6 * TILE]); b_dtw = B()
        dt_sb = sb("dtD", [NH, TILE]); b_dt = B()
        eac = sb("eacD", [NH, TILE]); b_eac = B()
        dA_sb = sb("dAD", [NH, TILE]); b_dA = B()

        for b in range(NB):
            pr.op("pool", lambda e: e.memset(cvcar[:], 0.0), writes=(b_cvc,))
            for it in range(NTL):
                t0 = it * TILE
                slot = b * NTL + it
                pr.ld(lambda e, b=b, t0=t0, d=x1t.rearrange("p (c t) -> p c t", c=DC):
                      e.dma_start(d, g.x1_scr[b, :, :, t0:t0 + TILE].rearrange("c p t -> p c t")),
                      reads=(dbuf(g.x1_scr),), writes=(b_x1t,))
                g.emit_ln(x1t, b_x1t, sq, b_sql, rowt, b_rowt, bc, b_bc, g.st_ln2, slot)
                if dbg:
                    pr.st(lambda e, slot=slot: e.dma_start(io["dbg_rowtD"][slot], rowt[:, :]),
                          reads=(b_rowt,), writes=(dbuf(io["dbg_rowtD"]),))
                    pr.st(lambda e, slot=slot: e.dma_start(io["dbg_bcD"][slot], bc[:, :]),
                          reads=(b_bc,), writes=(dbuf(io["dbg_bcD"]),))
                    pr.st(lambda e, slot=slot: e.dma_start(io["dbg_x1t"][slot], x1t[:, 0:TILE]),
                          reads=(b_x1t,), writes=(dbuf(io["dbg_x1t"]),))
                    pr.st(lambda e, slot=slot: e.dma_start(io["dbg_ones"][slot], ones_f[:, :]),
                          reads=(b_rowt,), writes=(dbuf(io["dbg_ones"]),))
                for c in range(DC):
                    pr.op("dve", lambda e, c=c: e.tensor_tensor(
                        xtmp[:], x1t[:, c * TILE:(c + 1) * TILE], bc[:, 0:TILE], ALU.mult),
                        reads=(b_x1t, b_bc), writes=(b_xtmp,))
                    pr.op("dve", lambda e, c=c: e.tensor_tensor(
                        xn1[:, c * TILE:(c + 1) * TILE], xtmp[:], bc[:, TILE:2 * TILE],
                        ALU.subtract), reads=(b_xtmp, b_bc), writes=(b_xn1,))
                pr.st(lambda e, b=b, t0=t0: e.dma_start(
                    g.xn1_scr[b, :, :, t0:t0 + TILE].rearrange("c p t -> p c t"),
                    xn1.rearrange("p (c t) -> p c t", c=DC)),
                    reads=(b_xn1,), writes=(dbuf(g.xn1_scr),))
                for oc in (range(35) if KTRUNC < 8 else ()):
                    Mo = 128 if oc < 34 else 32
                    off = 0 if oc % 2 == 0 else 512
                    ps = PS1[:, off:off + 512]
                    bps = bPS1[0] if oc % 2 == 0 else bPS1[1]
                    def mmg(e, ps=ps, oc=oc, Mo=Mo):
                        for kc in range(DC):
                            mm = e.matmul(ps[0:Mo, 0:TILE],
                                          ipw[:, kc * DIP + oc * P:kc * DIP + oc * P + Mo],
                                          xn1[:, kc * TILE:(kc + 1) * TILE],
                                          start=(kc == 0), stop=(kc == DC - 1))
                        return mm
                    pr.op("pe", mmg, reads=(b_ipw, b_xn1), writes=(bps,))
                    if oc < 16:
                        rc = oc % 2
                        pr.op("act", lambda e, ps=ps, rc=rc: e.activation(
                            so[rc][:], ps[:, 0:TILE], AF.Silu), reads=(bps,), writes=(b_so[rc],))
                        pr.st(lambda e, b=b, oc=oc, t0=t0, rc=rc: e.dma_start(
                            g.sz_scr[b, oc, :, t0:t0 + TILE], so[rc][:]),
                            reads=(b_so[rc],), writes=(dbuf(g.sz_scr),))
                    elif oc < 34:
                        c = oc - 16
                        rc = c % 2
                        xb, b_xb = xbc[rc], b_xbc[rc]
                        cv, b_cv = cvt[rc], b_cvt[rc]
                        pr.op("dve", lambda e, c=c, xb=xb: e.tensor_copy(
                            xb[:, 0:3], cvcar[:, c * 3:(c + 1) * 3]),
                            reads=(b_cvc,), writes=(b_xb,))
                        pr.op("act", lambda e, ps=ps, xb=xb: e.activation(
                            xb[:, 3:TILE + 3], ps[:, 0:TILE], AF.Copy),
                            reads=(bps,), writes=(b_xb,))
                        pr.op("dve", lambda e, c=c, xb=xb: e.tensor_copy(
                            cvcar[:, c * 3:(c + 1) * 3], xb[:, TILE:TILE + 3]),
                            reads=(b_xb,), writes=(b_cvc,))
                        pr.op("dve", lambda e, c=c, xb=xb, cv=cv: e.tensor_scalar(
                            cv[:], xb[:, 0:TILE], g.cw1[0][:, c:c + 1], None, ALU.mult),
                            reads=(b_xb, g.b_cw1[0]), writes=(b_cv,))
                        for j in range(1, 4):
                            pr.op("dve", lambda e, c=c, j=j, xb=xb, cv=cv: e.scalar_tensor_tensor(
                                cv[:], xb[:, j:TILE + j], g.cw1[j][:, c:c + 1], cv[:],
                                ALU.mult, ALU.add),
                                reads=(b_xb, g.b_cw1[j]), writes=(b_cv,))
                        pr.op("act", lambda e, c=c, cv=cv, rc=rc: e.activation(
                            so[rc][:], cv[:], AF.Silu, bias=g.cb1[:, c:c + 1]),
                            reads=(b_cv, g.b_cb1), writes=(b_so[rc],))
                        if c < DINC:
                            pr.st(lambda e, b=b, c=c, t0=t0, rc=rc: e.dma_start(
                                g.xs_scr[b, c, :, t0:t0 + TILE], so[rc][:]),
                                reads=(b_so[rc],), writes=(dbuf(g.xs_scr),))
                        else:
                            pr.st(lambda e, b=b, c=c, t0=t0, rc=rc: e.dma_start(
                                g.bc_scr[b, c - DINC, :, t0:t0 + TILE], so[rc][:]),
                                reads=(b_so[rc],), writes=(dbuf(g.bc_scr),))
                    else:
                        pr.op("act", lambda e, ps=ps: e.activation(
                            dtw[:, 0:TILE], ps[0:NH, 0:TILE], AF.Relu, bias=g.dtb[:, 0:1]),
                            reads=(bps, g.b_dtb), writes=(b_dtw,))
                        pr.op("act", lambda e, ps=ps: e.activation(
                            dtw[:, TILE:2 * TILE], ps[0:NH, 0:TILE], AF.Abs, bias=g.dtb[:, 0:1]),
                            reads=(bps, g.b_dtb), writes=(b_dtw,))
                        pr.op("act", lambda e: e.activation(
                            dtw[:, 2 * TILE:3 * TILE], dtw[:, TILE:2 * TILE], AF.Exp, scale=-1.0),
                            reads=(), writes=(b_dtw,))
                        pr.op("act", lambda e: e.activation(
                            dtw[:, 3 * TILE:4 * TILE], dtw[:, 2 * TILE:3 * TILE], AF.Ln, bias=1.0),
                            reads=(), writes=(b_dtw,))
                        pr.op("dve", lambda e: e.tensor_tensor(
                            dt_sb[:], dtw[:, 0:TILE], dtw[:, 3 * TILE:4 * TILE], ALU.add),
                            reads=(b_dtw,), writes=(b_dt,))
                        pr.op("dve", lambda e: e.tensor_scalar(
                            dtw[:, 4 * TILE:5 * TILE], dt_sb[:], g.negA[:, 0:1], None, ALU.mult),
                            reads=(b_dt, g.b_negA), writes=(b_dtw,))
                        pr.op("act", lambda e: e.activation(
                            dA_sb[:], dtw[:, 4 * TILE:5 * TILE], AF.Exp),
                            reads=(b_dtw,), writes=(b_dA,))
                        for cc in range(NCC):
                            pr.op("dve", lambda e, cc=cc: e.tensor_tensor_scan(
                                eac[:, cc * QC:(cc + 1) * QC],
                                ones_f[0:NH, 0:1].broadcast_to([NH, QC]),
                                dA_sb[:, cc * QC:(cc + 1) * QC], 1.0,
                                ALU.mult, ALU.mult),
                                reads=(b_dA, b_onesf), writes=(b_eac,))
                        pr.st(lambda e, slot=slot: e.dma_start(g.st_da[slot, :, :], dA_sb[:]),
                              reads=(b_dA,), writes=(dbuf(g.st_da),))
                        pr.st(lambda e, slot=slot: e.dma_start(g.st_dt[slot, :, :], dt_sb[:]),
                              reads=(b_dt,), writes=(dbuf(g.st_dt),))
                        pr.st(lambda e, slot=slot: e.dma_start(g.st_ea[slot, :, :], eac[:]),
                              reads=(b_eac,), writes=(dbuf(g.st_ea),))
                        if dbg:
                            pr.st(lambda e, slot=slot: e.dma_start(
                                io["dbg_ones2"][slot], ones_f[:, :]),
                                reads=(b_eac,), writes=(dbuf(io["dbg_ones2"]),))
                if int(os.environ.get("KTBAR", "0")):
                    pr.barrier()
    pr.barrier()

    if KTRUNC == 9:
        return
    with ExitStack() as esD2:
     if not g.need_d:
        pass
     else:
        sb = lambda n, s_, d=F32: esD2.enter_context(nc.sbuf_tensor(n, s_, d))
        opw = sb("opwD", [P, DINC * D], BF16); b_opw = B()
        for kc in range(DINC):
            pr.ld(lambda e, kc=kc, d=opw[:, kc * D:(kc + 1) * D]:
                  e.dma_start(d, g.opw_scr[kc * P:(kc + 1) * P, :]),
                  reads=(dbuf(g.opw_scr),), writes=(b_opw,))
        szt = sb("sztD", [P, DINC * TILE], BF16); b_szt = B()
        xst = sb("xstD", [P, DINC * TILE], BF16); b_xst = B()
        Bt = sb("BtD", [P, TILE], BF16); b_Bt = B()
        Ct = sb("CtD", [P, TILE], BF16); b_Ct = B()
        xn1t = sb("xn1tD", [P, DC * TILE], BF16); b_xn1t = B()
        dt_t = sb("dt_tD", [NH, TILE]); b_dtt = B()
        eac_t = sb("eac_tD", [NH, TILE]); b_eact = B()
        dabc = sb("dabcD", [P, 2048]); b_dabc = B()
        stk = sb("stkD", [64, P]); b_stk = B()
        adT = sb("adTD", [P, 64]); b_adT = B()
        G_bf = sb("GbfD", [P, P], BF16); b_G = B()
        BT_bf = sb("BTbfD", [P, P], BF16); b_BT = B()
        decT = sb("decTD", [P, NH], BF16); b_decT = B()
        Mh = sb("MhD", [P, 2048], BF16); b_Mh = B()
        GM = sb("GMD", [P, NH * QC], BF16); b_GM = B()
        dtxT = sb("dtxTD", [P, 2048], BF16); b_dtxT = B()
        dtxdec = sb("dtxdecD", [P, 2048], BF16); b_dtxdec = B()
        xD = sb("xDD", [P, 2048], BF16); b_xD = B()
        S_all = sb("SallD", [P, 2048]); b_S = B()
        S_bf = sb("SbfD", [P, 2048], BF16); b_Sbf = B()
        Smid = sb("SmidD", [P, 2048]); b_Smid = B()
        Yt = sb("YtD", [P, 2048], BF16); b_Yt = B()
        yz = sb("yzD", [P, DINC * TILE], BF16); b_yz = B()
        ct1 = sb("ct1D", [P, 1024]); b_ct1 = B()
        cdbc = sb("cdbcD", [P, NH]); b_cdbc = B()
        identw = sb("identwD", [P, 2048]); b_idw = B()
        pr.op("dve", lambda e: e.tensor_copy(
            identw.rearrange("p (h q) -> p h q", h=16),
            ap(ident_f, 0, [[P, P], [0, 16], [1, P]])),
            reads=(ident_f is None and () or (g.b_identf,))[0] if False else (g.b_identf,),
            writes=(b_idw,))
        sqb = [sb(f"sqbD{r}", [P, TILE], BF16) for r in range(2)]; b_sqb = pr.bufs(2)
        rrow = sb("rrowD", [1, TILE]); b_rrow = B()
        bcr = sb("bcrD", [P, TILE]); b_bcr = B()
        x1c = [sb(f"x1cD{r}", [P, TILE]) for r in range(2)]; b_x1c = pr.bufs(2)
        ta = [sb(f"taD{r}", [P, TILE]) for r in range(2)]; b_ta = pr.bufs(2)
        px2c = [sb(f"px2cD{r}", [P, TILE]) for r in range(2)]; b_px2c = pr.bufs(2)

        for b in range(NB):
            pr.op("pool", lambda e: e.memset(S_all[:], 0.0), writes=(b_S,))
            pr.op("pool", lambda e: e.memset(S_bf[:], 0.0), writes=(b_Sbf,))
            for it in range(NTL):
                t0 = it * TILE
                slot = b * NTL + it
                pr.ld(lambda e, b=b, t0=t0, d=szt.rearrange("p (c t) -> p c t", c=DINC):
                      e.dma_start(d, g.sz_scr[b, :, :, t0:t0 + TILE].rearrange("c p t -> p c t")),
                      reads=(dbuf(g.sz_scr),), writes=(b_szt,))
                pr.ld(lambda e, b=b, t0=t0, d=xst.rearrange("p (c t) -> p c t", c=DINC):
                      e.dma_start(d, g.xs_scr[b, :, :, t0:t0 + TILE].rearrange("c p t -> p c t")),
                      reads=(dbuf(g.xs_scr),), writes=(b_xst,))
                pr.ld(lambda e, b=b, t0=t0: e.dma_start(Bt[:, :], g.bc_scr[b, 0, :, t0:t0 + TILE]),
                      reads=(dbuf(g.bc_scr),), writes=(b_Bt,))
                pr.ld(lambda e, b=b, t0=t0: e.dma_start(Ct[:, :], g.bc_scr[b, 1, :, t0:t0 + TILE]),
                      reads=(dbuf(g.bc_scr),), writes=(b_Ct,))
                pr.ld(lambda e, b=b, t0=t0, d=xn1t.rearrange("p (c t) -> p c t", c=DC):
                      e.dma_start(d, g.xn1_scr[b, :, :, t0:t0 + TILE].rearrange("c p t -> p c t")),
                      reads=(dbuf(g.xn1_scr),), writes=(b_xn1t,))
                pr.ld(lambda e, slot=slot: e.dma_start(dt_t[:, :], g.st_dt[slot, :, :]),
                      reads=(dbuf(g.st_dt),), writes=(b_dtt,))
                pr.ld(lambda e, slot=slot: e.dma_start(eac_t[:, :], g.st_ea[slot, :, :]),
                      reads=(dbuf(g.st_ea),), writes=(b_eact,))
                for cc in range(NCC):
                    slotcc = slot * NCC + cc
                    pr.op("dve", lambda e, cc=cc: e.tensor_copy(
                        stk[0:NH, :], dt_t[:, cc * QC:(cc + 1) * QC]),
                        reads=(b_dtt,), writes=(b_stk,))
                    pr.op("dve", lambda e, cc=cc: e.tensor_copy(
                        stk[NH:2 * NH, :], eac_t[:, cc * QC:(cc + 1) * QC]),
                        reads=(b_eact,), writes=(b_stk,))
                    pr.op("pe", lambda e: e.transpose(PSA[:, 0:64], stk[:, :], ident_f[0:64, 0:64]),
                          reads=(b_stk, b_identf), writes=(bPSA,))
                    pr.op("act", lambda e: e.activation(adT[:, :], PSA[:, 0:64], AF.Copy),
                          reads=(bPSA,), writes=(b_adT,))
                    pr.op("pe", lambda e, cc=cc: e.matmul(
                        PSB[:, 0:P], Bt[:, cc * QC:(cc + 1) * QC], Ct[:, cc * QC:(cc + 1) * QC],
                        start=True, stop=True), reads=(b_Bt, b_Ct), writes=(bPSB,))
                    pr.op("act", lambda e: e.activation(G_bf[:, :], PSB[:, 0:P], AF.Copy),
                          reads=(bPSB,), writes=(b_G,))
                    pr.op("pe", lambda e, cc=cc: e.transpose(
                        psbbf[:, 512:640], Bt[:, cc * QC:(cc + 1) * QC], ident_b[:]),
                        reads=(b_Bt, b_identb), writes=(bPSB,))
                    pr.op("act", lambda e: e.activation(BT_bf[:, :], psbbf[:, 512:640], AF.Copy),
                          reads=(bPSB,), writes=(b_BT,))
                    for chk in range(DINC):
                        pr.op("pe", lambda e, chk=chk, cc=cc: e.transpose(
                            ps1bf[chk // 8][:, (chk % 8) * P:(chk % 8 + 1) * P],
                            xst[:, chk * TILE + cc * QC:chk * TILE + (cc + 1) * QC], ident_b[:]),
                            reads=(b_xst, b_identb), writes=(bPS1[chk // 8],))
                    for hh in range(2):
                        pr.op("dve", lambda e, hh=hh: e.tensor_tensor(
                            dtxT[:, hh * 1024:(hh + 1) * 1024].rearrange("p (h q) -> p h q", h=16),
                            ps1bf[hh].rearrange("p (h q) -> p h q", h=16),
                            ap(adT, hh * 16, [[64, P], [1, 16], [0, HD]]), ALU.mult),
                            reads=(bPS1[hh], b_adT), writes=(b_dtxT,))
                        pr.op("dve", lambda e, hh=hh: e.tensor_tensor(
                            xD[:, hh * 1024:(hh + 1) * 1024].rearrange("p (h q) -> p h q", h=16),
                            ps1bf[hh].rearrange("p (h q) -> p h q", h=16),
                            ap(g.Dbc, hh * 16, [[NH, P], [1, 16], [0, HD]]), ALU.mult),
                            reads=(bPS1[hh], g.b_Dbc), writes=(b_xD,))
                    for hh in range(2):
                        pr.ld(lambda e, slot=slot, cc=cc, hh=hh: e.dma_start(
                            dabc.rearrange("p (h q) -> p h q", h=16),
                            ap(g.st_da, slot * NH * TILE + (hh * 16) * TILE + cc * QC,
                               [[0, P], [TILE, 16], [1, QC]])),
                            reads=(dbuf(g.st_da),), writes=(b_dabc,))
                        pr.op("dve", lambda e: e.tensor_tensor_scan(
                            Mh[:, :], dabc[:, :], identw[:, :], 0.0,
                            ALU.mult, ALU.add),
                            reads=(b_dabc, b_idw), writes=(b_Mh,))
                        pr.op("dve", lambda e, hh=hh: e.tensor_copy(
                            decT[:, hh * 16:(hh + 1) * 16],
                            ap(Mh, QC - 1, [[2048, P], [QC, 16]])),
                            reads=(b_Mh,), writes=(b_decT,))
                        pr.op("pool", lambda e, hh=hh: e.tensor_tensor(
                            GM[:, hh * 2048:(hh + 1) * 2048].rearrange("p (h q) -> p h q", h=16),
                            Mh.rearrange("p (h q) -> p h q", h=16),
                            ap(G_bf, 0, [[P, P], [0, 16], [1, P]]), ALU.mult),
                            reads=(b_Mh, b_G), writes=(b_GM,))
                    pr.op("pool", lambda e: e.tensor_tensor(
                        dtxdec.rearrange("p (h q) -> p h q", h=NH),
                        dtxT.rearrange("p (h q) -> p h q", h=NH),
                        ap(decT, 0, [[NH, P], [1, NH], [0, HD]]), ALU.mult),
                        reads=(b_dtxT, b_decT), writes=(b_dtxdec,))
                    def mm_intra(e):
                        for q in range(4):
                            e.matmul(PS2[:, q * TILE:(q + 1) * TILE], ident_b[:, :],
                                     xD[:, q * TILE:(q + 1) * TILE], start=True, stop=False)
                        for h in range(NH):
                            mm = e.matmul(PS2[:, h * HD:(h + 1) * HD], GM[:, h * QC:(h + 1) * QC],
                                          dtxT[:, h * HD:(h + 1) * HD],
                                          start=False, stop=(h % 8 == 7))
                        return mm
                    pr.op("pe", mm_intra, reads=(b_identb, b_xD, b_GM, b_dtxT),
                          writes=tuple(bPS2))
                    for hh in range(2):
                        def mm_inter(e, hh=hh, cc=cc):
                            for q in range(2):
                                mm = e.matmul(PS1[:, q * TILE:(q + 1) * TILE],
                                              Ct[:, cc * QC:(cc + 1) * QC],
                                              S_bf[:, hh * 1024 + q * TILE:hh * 1024 + (q + 1) * TILE],
                                              start=True, stop=True)
                            return mm
                        pr.op("pe", mm_inter, reads=(b_Ct, b_Sbf), writes=tuple(bPS1))
                        pr.op("dve", lambda e, hh=hh: e.tensor_tensor(
                            ct1.rearrange("p (h q) -> p h q", h=16),
                            PS1[:, 0:1024].rearrange("p (h q) -> p h q", h=16),
                            ap(adT, 32 + hh * 16, [[64, P], [1, 16], [0, HD]]), ALU.mult),
                            reads=tuple(bPS1) + (b_adT,), writes=(b_ct1,))
                        pr.op("dve", lambda e, hh=hh: e.tensor_tensor(
                            Yt[:, hh * 1024:(hh + 1) * 1024], ct1[:, :],
                            PS2[:, hh * 1024:(hh + 1) * 1024], ALU.add),
                            reads=(b_ct1, bPS2[2 * hh], bPS2[2 * hh + 1]), writes=(b_Yt,))
                    def mm_state(e, cc=cc):
                        for q in range(4):
                            mm = e.matmul(PS2[:, q * TILE:(q + 1) * TILE], BT_bf[:, :],
                                          dtxdec[:, q * TILE:(q + 1) * TILE], start=True, stop=True)
                        return mm
                    pr.op("pe", mm_state, reads=(b_BT, b_dtxdec), writes=tuple(bPS2))
                    sb_ = dbuf(g.st_cd)
                    pr.st(lambda e, slotcc=slotcc, cc=cc: e.dma_start(
                        ap(g.st_cd, slotcc * NH, [[1, NH], [1, 1]]),
                        eac_t[:, cc * QC + QC - 1:cc * QC + QC],
                        allow_slow_non_contiguous=True),
                        reads=(b_eact,), writes=(sb_,))
                    pr.ld(lambda e, slotcc=slotcc: e.dma_start(
                        cdbc[:, :], ap(g.st_cd, slotcc * NH, [[0, P], [1, NH]])),
                        reads=(sb_,), writes=(b_cdbc,))
                    pr.op("dve", lambda e: e.tensor_tensor(
                        Smid.rearrange("p (h q) -> p h q", h=NH),
                        S_all.rearrange("p (h q) -> p h q", h=NH),
                        ap(cdbc, 0, [[NH, P], [1, NH], [0, HD]]), ALU.mult),
                        reads=(b_S, b_cdbc), writes=(b_Smid,))
                    pr.op("dve", lambda e: e.tensor_tensor(
                        S_all[:, :], Smid[:, :], PS2[:, :], ALU.add),
                        reads=(b_Smid,) + tuple(bPS2), writes=(b_S,))
                    pr.op("dve", lambda e: e.tensor_copy(S_bf[:], S_all[:]),
                          reads=(b_S,), writes=(b_Sbf,))
                    for chk in range(DINC):
                        dstv = psabf if chk < 8 else psbbf
                        dstb = bPSA if chk < 8 else bPSB
                        pr.op("pe", lambda e, chk=chk, dstv=dstv: e.transpose(
                            dstv[:, (chk % 8) * P:(chk % 8 + 1) * P],
                            Yt[:, chk * P:(chk + 1) * P], ident_b[:]),
                            reads=(b_Yt, b_identb), writes=(dstb,))
                    for hh in range(2):
                        srcv = psabf if hh == 0 else psbbf
                        srcb = bPSA if hh == 0 else bPSB
                        pr.op("dve", lambda e, hh=hh, cc=cc, srcv=srcv: e.tensor_tensor(
                            ap(yz, cc * QC + hh * 8 * TILE, [[DINC * TILE, P], [TILE, 8], [1, QC]]),
                            srcv.rearrange("p (c q) -> p c q", c=8),
                            ap(szt, cc * QC + hh * 8 * TILE, [[DINC * TILE, P], [TILE, 8], [1, QC]]),
                            ALU.mult),
                            reads=(srcb, b_szt), writes=(b_yz,))
                for chk in range(DINC):
                    rc = chk % 2
                    pr.op("act", lambda e, chk=chk, rc=rc: e.activation(
                        sqb[rc][:], yz[:, chk * TILE:(chk + 1) * TILE], AF.Square),
                        reads=(b_yz,), writes=(b_sqb[rc],))
                    pr.op("pe", lambda e, chk=chk, rc=rc: e.matmul(
                        PSA[0:1, 0:TILE], ones_b[:, 0:1], sqb[rc][:],
                        start=(chk == 0), stop=(chk == DINC - 1)),
                        reads=(b_sqb[rc], b_onesb), writes=(bPSA,))
                pr.op("act", lambda e: e.activation(
                    rrow[0:1, :], PSA[0:1, 0:TILE], AF.Ln, bias=EPS, scale=1.0 / DIN),
                    reads=(bPSA,), writes=(b_rrow,))
                pr.op("act", lambda e: e.activation(
                    rrow[0:1, :], rrow[0:1, :], AF.Exp, scale=-0.5),
                    reads=(), writes=(b_rrow,))
                g.bounce_bcast(rrow[0:1, :], b_rrow, g.st_rms, slot * TILE, TILE, bcr, b_bcr)
                for chk in range(DINC):
                    pr.op("dve", lambda e, chk=chk: e.scalar_tensor_tensor(
                        yz[:, chk * TILE:(chk + 1) * TILE], yz[:, chk * TILE:(chk + 1) * TILE],
                        g.nw[:, chk:chk + 1], bcr[:, :], ALU.mult, ALU.mult),
                        reads=(g.b_nw, b_bcr), writes=(b_yz,))
                for eo in range(DC):
                    q = eo % 4
                    psap = PS2[:, q * TILE:(q + 1) * TILE]
                    bq = bPS2[q]
                    def mm_op(e, psap=psap, eo=eo):
                        for kc in range(DINC):
                            mm = e.matmul(psap, opw[:, kc * D + eo * P:kc * D + (eo + 1) * P],
                                          yz[:, kc * TILE:(kc + 1) * TILE],
                                          start=(kc == 0), stop=(kc == DINC - 1))
                        return mm
                    pr.op("pe", mm_op, reads=(b_opw, b_yz), writes=(bq,))
                    rc = eo % 2
                    pr.ld(lambda e, b=b, eo=eo, t0=t0, rc=rc: e.dma_start(
                        x1c[rc][:], g.x1_scr[b, eo, :, t0:t0 + TILE]),
                        reads=(dbuf(g.x1_scr),), writes=(b_x1c[rc],))
                    pr.op("dve", lambda e, psap=psap, rc=rc: e.scalar_tensor_tensor(
                        ta[rc][:], psap, 1.0, x1c[rc][:], ALU.mult, ALU.add),
                        reads=(bq, b_x1c[rc]), writes=(b_ta[rc],))
                    pr.op("dve", lambda e, eo=eo, rc=rc: e.tensor_tensor(
                        px2c[rc][:], ta[rc][:], xn1t[:, eo * TILE:(eo + 1) * TILE], ALU.add),
                        reads=(b_ta[rc], b_xn1t), writes=(b_px2c[rc],))
                    pr.st(lambda e, b=b, eo=eo, t0=t0, rc=rc: e.dma_start(
                        g.px2_scr[b, eo, :, t0:t0 + TILE], px2c[rc][:]),
                        reads=(b_px2c[rc],), writes=(dbuf(g.px2_scr),))
    pr.barrier()

    if KTRUNC == 10:
        return
    with ExitStack() as esE1:
     if not g.need_e1:
        pass
     else:
        sb = lambda n, s_, d=F32: esE1.enter_context(nc.sbuf_tensor(n, s_, d))
        hf = sb("hfE", [P, DC * T], BF16); b_hf = B()
        xn1t = sb("xn1tE", [P, DC * TILE], BF16); b_xt = B()
        sqb = [sb(f"sqbE{r}", [P, TILE], BF16) for r in range(2)]; b_sqb = pr.bufs(2)
        rowt = g.rowt_sh; b_rowt = g.b_rowt_sh
        bc = g.bc_sh; b_bc = g.b_bc_sh
        xtmp = sb("xtmpE", [P, TILE]); b_xtmp = B()
        w1t = [sb(f"w1E{r}", [P, DC * P], BF16) for r in range(2)]; b_w1t = pr.bufs(2)
        hm = [sb(f"hmE{r}", [P, T], BF16) for r in range(2)]; b_hm = pr.bufs(2)
        for b in range(NB):
            for it in range(NTL):
                t0 = it * TILE
                slot = b * NTL + it
                pr.ld(lambda e, b=b, t0=t0, d=xn1t.rearrange("p (c t) -> p c t", c=DC):
                      e.dma_start(d, g.xn1_scr[b, :, :, t0:t0 + TILE].rearrange("c p t -> p c t")),
                      reads=(dbuf(g.xn1_scr),), writes=(b_xt,))
                for c in range(DC):
                    pr.op("pe", lambda e, c=c: e.matmul(
                        PSA[0:1, 0:TILE], ones_b[:, 0:1], xn1t[:, c * TILE:(c + 1) * TILE],
                        start=(c == 0), stop=(c == DC - 1)),
                        reads=(b_xt, b_onesb), writes=(bPSA,))
                for c in range(DC):
                    r = c % 2
                    pr.op("act", lambda e, c=c, r=r: e.activation(
                        sqb[r][:], xn1t[:, c * TILE:(c + 1) * TILE], AF.Square),
                        reads=(b_xt,), writes=(b_sqb[r],))
                    pr.op("pe", lambda e, r=r, c=c: e.matmul(
                        PSB[0:1, 0:TILE], ones_b[:, 0:1], sqb[r][:],
                        start=(c == 0), stop=(c == DC - 1)),
                        reads=(b_sqb[r], b_onesb), writes=(bPSB,))
                pr.op("act", lambda e: e.activation(rowt[0:1, 0:TILE], PSA[0:1, 0:TILE], AF.Copy, scale=1.0 / D),
                      reads=(bPSA,), writes=(b_rowt,))
                pr.op("act", lambda e: e.activation(rowt[0:1, TILE:2 * TILE], PSB[0:1, 0:TILE], AF.Copy, scale=1.0 / D),
                      reads=(bPSB,), writes=(b_rowt,))
                pr.op("dve", lambda e: e.tensor_tensor(
                    rowt[0:1, 2 * TILE:3 * TILE], rowt[0:1, 0:TILE], rowt[0:1, 0:TILE], ALU.mult),
                    reads=(), writes=(b_rowt,))
                pr.op("dve", lambda e: e.tensor_tensor(
                    rowt[0:1, 3 * TILE:4 * TILE], rowt[0:1, TILE:2 * TILE],
                    rowt[0:1, 2 * TILE:3 * TILE], ALU.subtract), reads=(), writes=(b_rowt,))
                pr.op("act", lambda e: e.activation(
                    rowt[0:1, 2 * TILE:3 * TILE], rowt[0:1, 3 * TILE:4 * TILE], AF.Ln, bias=EPS),
                    reads=(), writes=(b_rowt,))
                pr.op("act", lambda e: e.activation(
                    rowt[0:1, 2 * TILE:3 * TILE], rowt[0:1, 2 * TILE:3 * TILE], AF.Exp, scale=-0.5),
                    reads=(), writes=(b_rowt,))
                pr.op("dve", lambda e: e.tensor_tensor(
                    rowt[0:1, 3 * TILE:4 * TILE], rowt[0:1, 0:TILE], rowt[0:1, 2 * TILE:3 * TILE],
                    ALU.mult), reads=(), writes=(b_rowt,))
                g.bounce_bcast(rowt[0:1, 2 * TILE:4 * TILE], b_rowt, g.st_ln, slot * 1024, 1024, bc, b_bc)
                for c in range(DC):
                    pr.op("dve", lambda e, c=c: e.tensor_tensor(
                        xtmp[:], xn1t[:, c * TILE:(c + 1) * TILE], bc[:, 0:TILE], ALU.mult),
                        reads=(b_xt, b_bc), writes=(b_xtmp,))
                    pr.op("dve", lambda e, c=c: e.tensor_tensor(
                        xtmp[:], xtmp[:], bc[:, TILE:2 * TILE], ALU.subtract),
                        reads=(b_bc,), writes=(b_xtmp,))
                    pr.op("dve", lambda e, c=c, b=b, t0=t0: e.tensor_scalar(
                        hf[:, c * T + t0:c * T + t0 + TILE], xtmp[:],
                        g.lng[:, c:c + 1], g.lnb[:, c:c + 1], ALU.mult, ALU.add),
                        reads=(b_xtmp, g.b_lng, g.b_lnb), writes=(b_hf,))
            for hb in range(HC):
                r = hb % 2
                pr.ld(lambda e, hb=hb, r=r, d=w1t[r].rearrange("p (k c) -> p k c", k=DC):
                      e.dma_start(d, g.fc1d_scr[:, hb * P:(hb + 1) * P].rearrange("(k p) c -> p k c", p=P)),
                      reads=(dbuf(g.fc1d_scr),), writes=(b_w1t[r],))
                for tj in range(NTL):
                    psx, bpsx = (PS2[:, 0:TILE], bPS2[0]) if tj % 2 == 0 else (PS2[:, TILE:2 * TILE], bPS2[1])
                    def mmg(e, psx=psx, r=r, tj=tj):
                        for kc in range(DC):
                            mm = e.matmul(psx, w1t[r][:, kc * P:(kc + 1) * P],
                                          hf[:, kc * T + tj * TILE:kc * T + (tj + 1) * TILE],
                                          start=(kc == 0), stop=(kc == DC - 1))
                        return mm
                    pr.op("pe", mmg, reads=(b_w1t[r], b_hf), writes=(bpsx,))
                    pr.op("act", lambda e, psx=psx, hb=hb, r=r, tj=tj: e.activation(
                        hm[r][:, tj * TILE:(tj + 1) * TILE], psx, AF.Silu, bias=g.fc1b[:, hb:hb + 1]),
                        reads=(bpsx, g.b_fc1b), writes=(b_hm[r],))
                pr.st(lambda e, b=b, hb=hb, r=r: e.dma_start(g.hm_scr[b, hb, :, :], hm[r][:]),
                      reads=(b_hm[r],), writes=(dbuf(g.hm_scr),))
    pr.barrier()

    if KTRUNC == 11:
        return
    with ExitStack() as esE2:
     if not g.need_e2:
        pass
     else:
        sb = lambda n, s_, d=F32: esE2.enter_context(nc.sbuf_tensor(n, s_, d))
        w2sb = sb("w2E", [P, HC * D], BF16); b_w2 = B()
        for kc in range(HC):
            pr.ld(lambda e, kc=kc, d=w2sb[:, kc * D:(kc + 1) * D]:
                  e.dma_start(d, g.fc2d_scr[kc * P:(kc + 1) * P, :]),
                  reads=(dbuf(g.fc2d_scr),), writes=(b_w2,))
        hmt = sb("hmtE", [P, HC * TILE], BF16); b_hmt = B()
        tc = [sb(f"tcE{r}", [P, TILE]) for r in range(2)]; b_tc = pr.bufs(2)
        pxc = [sb(f"pxcE{r}", [P, TILE]) for r in range(2)]; b_pxc = pr.bufs(2)
        x2c = [sb(f"x2cE{r}", [P, TILE]) for r in range(2)]; b_x2c = pr.bufs(2)
        for b in range(NB):
            for it in range(NTL):
                t0 = it * TILE
                pr.ld(lambda e, b=b, t0=t0, d=hmt.rearrange("p (c t) -> p c t", c=HC):
                      e.dma_start(d, g.hm_scr[b, :, :, t0:t0 + TILE].rearrange("c p t -> p c t")),
                      reads=(dbuf(g.hm_scr),), writes=(b_hmt,))
                for eo in range(DC):
                    q = eo % 4
                    psx, bq = PS2[:, q * TILE:(q + 1) * TILE], bPS2[q]
                    def mmg(e, psx=psx, eo=eo):
                        for kc in range(HC):
                            mm = e.matmul(psx, w2sb[:, kc * D + eo * P:kc * D + (eo + 1) * P],
                                          hmt[:, kc * TILE:(kc + 1) * TILE],
                                          start=(kc == 0), stop=(kc == HC - 1))
                        return mm
                    pr.op("pe", mmg, reads=(b_w2, b_hmt), writes=(bq,))
                    rc = eo % 2
                    pr.ld(lambda e, b=b, eo=eo, t0=t0, rc=rc: e.dma_start(
                        pxc[rc][:], g.px2_scr[b, eo, :, t0:t0 + TILE]),
                        reads=(dbuf(g.px2_scr),), writes=(b_pxc[rc],))
                    pr.op("dve", lambda e, psx=psx, rc=rc: e.scalar_tensor_tensor(
                        tc[rc][:], psx, 0.5, pxc[rc][:], ALU.mult, ALU.add),
                        reads=(bq, b_pxc[rc]), writes=(b_tc[rc],))
                    pr.op("dve", lambda e, eo=eo, rc=rc: e.tensor_scalar(
                        x2c[rc][:], tc[rc][:], g.fc2b05[:, eo:eo + 1], None, ALU.add),
                        reads=(b_tc[rc], g.b_fc2b), writes=(b_x2c[rc],))
                    pr.st(lambda e, b=b, eo=eo, t0=t0, rc=rc: e.dma_start(
                        g.x2_out[b, eo, :, t0:t0 + TILE], x2c[rc][:]),
                        reads=(b_x2c[rc],), writes=(dbuf(g.x2_out),))
    pr.barrier()


def _host_l0(inputs):
    c = np.ascontiguousarray
    sh = {}
    sh["wrT"] = c(np.asarray(inputs["L0_attn_W_r"]).T)
    sh["wkT"] = c(np.asarray(inputs["L0_attn_W_k"]).T)
    sh["wvT"] = c(np.asarray(inputs["L0_attn_W_v"]).T)
    sh["woT"] = c(np.asarray(inputs["L0_attn_W_o"]).T)
    for nm, key in (("mixr", "L0_attn_time_mix_r"), ("mixk", "L0_attn_time_mix_k"),
                    ("mixv", "L0_attn_time_mix_v")):
        sh[nm] = c(np.asarray(inputs[key]).reshape(DC, P).T)
    sh["td"] = c(np.asarray(inputs["L0_attn_time_decay"]).reshape(DC, P).T)
    sh["tf"] = c(np.asarray(inputs["L0_attn_time_first"]).reshape(DC, P).T)
    sh["gnw"] = c(np.asarray(inputs["L0_attn_gn_weight"]).reshape(DC, P).T)
    sh["gnb"] = c(np.asarray(inputs["L0_attn_gn_bias"]).reshape(DC, P).T)
    sh["w1T"] = c(np.asarray(inputs["L0_ffn_w1"]).T)
    sh["w2T"] = c(np.asarray(inputs["L0_ffn_w2"]).T)
    cw = np.asarray(inputs["L0_ffn_conv_w"])[:, 0, :]
    sh["cwl0"] = c(cw.reshape(HC, P, 3).transpose(2, 1, 0))
    sh["cbl0"] = c(np.asarray(inputs["L0_ffn_conv_b"]).reshape(HC, P).T)
    return sh


def _host_l1(inputs):
    c = np.ascontiguousarray
    sh = {}
    sh["ipwT"] = c(np.asarray(inputs["L1_attn_in_proj_w"]).T)
    sh["opwT"] = c(np.asarray(inputs["L1_attn_out_proj_w"]).T)
    cw1 = np.asarray(inputs["L1_attn_conv_w"])[:, 0, :]
    sh["cwl1"] = c(cw1.reshape(CONVC, P, 4).transpose(2, 1, 0))
    sh["cbl1"] = c(np.asarray(inputs["L1_attn_conv_b"]).reshape(CONVC, P).T)
    sh["dtb"] = c(np.asarray(inputs["L1_attn_dt_bias"], np.float32).reshape(NH, 1))
    sh["alog"] = c(np.asarray(inputs["L1_attn_A_log"], np.float32).reshape(NH, 1))
    sh["dd"] = c(np.asarray(inputs["L1_attn_D"], np.float32).reshape(1, NH))
    sh["nw"] = c(np.asarray(inputs["L1_attn_norm_w"]).reshape(DINC, P).T)
    sh["lng"] = c(np.asarray(inputs["L1_ffn_ln_gamma"]).reshape(DC, P).T)
    sh["lnb"] = c(np.asarray(inputs["L1_ffn_ln_beta"]).reshape(DC, P).T)
    sh["fc1q"] = c(np.asarray(inputs["L1_ffn_fc1_q"], np.int32))
    sh["fc1s"] = c(np.asarray(inputs["L1_ffn_fc1_scales"], np.float32))
    sh["fc1b"] = c(np.asarray(inputs["L1_ffn_fc1_bias"]).reshape(HC, P).T)
    sh["fc2q"] = c(np.asarray(inputs["L1_ffn_fc2_q"], np.int32))
    sh["fc2s"] = c(np.asarray(inputs["L1_ffn_fc2_scales"], np.float32))
    sh["fc2b"] = c(np.asarray(inputs["L1_ffn_fc2_bias"]).reshape(DC, P).T)
    return sh


def _host_inputs(inputs, kinds=("l0", "l1")):
    c = np.ascontiguousarray
    sh = {}
    sh["ident"] = np.eye(P, dtype=np.float32)
    ind2 = np.zeros((P, 2), np.float32)
    ind2[0:64, 0] = 1.0
    ind2[64:128, 1] = 1.0
    sh["ind2"] = ind2
    sh["ind2T"] = c(ind2.T)
    if "l0" in kinds:
        sh.update(_host_l0(inputs))
    if "l1" in kinds:
        sh.update(_host_l1(inputs))
    return sh


_L0_KEYS = ("wrT", "wkT", "wvT", "woT", "mixr", "mixk", "mixv", "td", "tf", "gnw", "gnb",
            "w1T", "w2T", "cwl0", "cbl0")
_D_KEYS = ("ipwT", "opwT", "cwl1", "cbl1", "dtb", "alog", "dd", "nw")
_E1_KEYS = ("lng", "lnb", "fc1q", "fc1s", "fc1b")
_E2_KEYS = ("fc2q", "fc2s", "fc2b")
_CONST_KEYS = ("ident", "ind2", "ind2T")


def run_chain(inputs, kinds_seq=("l0", "d12", "e1", "e2"), ncores=8):
    x = np.asarray(inputs["L0_attn_x"], np.float32)
    sh = _host_inputs(inputs, kinds=("l0", "l1"))
    percore = [dict() for _ in range(ncores)]
    for core in range(ncores):
        percore[core]["xfm"] = np.ascontiguousarray(
            x[core * NB:(core + 1) * NB].transpose(0, 2, 1).reshape(NB, DC, P, T))
    out = None
    for kind in kinds_seq:
        nc, es, io = build_program(kind=kind)
        innames = [n for n, k in io["_kinds"].items() if k == "in"]
        in_maps = []
        for core in range(ncores):
            m = {}
            for n in innames:
                if n in sh:
                    m[n] = sh[n]
                elif n in percore[core]:
                    m[n] = percore[core][n]
                else:
                    raise KeyError(f"launch {kind}: missing input {n}")
            in_maps.append(m)
        res = run_bass_kernel_spmd(nc, in_maps, core_ids=list(range(ncores)))
        es.close()
        for core in range(ncores):
            for n, v in res.results[core].items():
                percore[core][n] = np.asarray(v)
        out = res
    return percore


def kernel(**inputs):
    percore = run_chain(inputs)
    out = np.empty((16, T, D), np.float32)
    for core in range(8):
        x2 = percore[core]["x2"]
        out[core * NB:(core + 1) * NB] = x2.reshape(NB, D, T).transpose(0, 2, 1)
    return out
